# version 19
# baseline (speedup 1.0000x reference)
import contextlib
import os
import numpy as np
import ml_dtypes
import concourse.bass as bass
import concourse.mybir as mybir
from concourse.bass_utils import run_bass_kernel_spmd

F32 = mybir.dt.float32
BF16 = mybir.dt.bfloat16
AF = mybir.ActivationFunctionType
AX = mybir.AxisListType
ALU = mybir.AluOpType

S = 4096
D = 1024
NT5 = S // 512
NEG = -1.0e9
HID = 2730
ENGS = ("tensor", "vector", "scalar", "gpsimd", "sync")


class B:

    def __init__(self, nc):
        self.nc = nc
        self.q = {e: [] for e in ENGS}
        self.stk = contextlib.ExitStack()
        self.sems = {}
        self.cnt = {}
        self.flush = None
        self.epoch = 0
        self.sem_epoch = {}

    def sem(self, name):
        if name not in self.sems:
            self.sems[name] = self.stk.enter_context(self.nc.semaphore(name))
            self.cnt[name] = 0
        return self.sems[name]

    def sb(self, name, shape, dtype):
        cm = self.nc.sbuf_tensor(name, shape, dtype)
        t = cm.__enter__()
        self.cms = getattr(self, "cms", {})
        self.seq = getattr(self, "seq", 0) + 1
        self.cms[name] = (self.seq, cm)
        return t

    def free(self, *names):
        for n in sorted(names, key=lambda n: -self.cms[n][0]):
            _, cm = self.cms.pop(n)
            cm.__exit__(None, None, None)

    def barrier(self, tag):
        vals = []
        fl = self.flush
        pre = {nm: self.cnt.get(nm, 0) for nm in ("pe", "dve", "act", "pool")}
        xwait = {"tensor": ("dve", pre["dve"]), "vector": ("act", pre["act"]),
                 "scalar": ("pool", pre["pool"]), "gpsimd": ("pe", pre["pe"])}
        for eng, nm in (("tensor", "pe"), ("vector", "dve"), ("scalar", "act"),
                        ("gpsimd", "pool")):
            v = self.bump(nm)
            sems = self.sems
            wn, wv = xwait[eng]
            self.sem(wn)

            def tk(e, nm=nm, eng=eng, wn=wn, wv=wv):
                if wv:
                    e.wait_ge(sems[wn], wv)
                if eng == "tensor" or fl is None:
                    e.nop().then_inc(sems[nm], 1)
                    if hasattr(e, "_n"):
                        e._n += 1
                elif eng == "scalar":
                    e.activation(fl[:, 2:4], fl[:, 2:4], AF.Copy)
                elif eng == "vector":
                    e.memset(fl[:, 0:2], 0.0)
                else:
                    e.memset(fl[:, 4:6], 0.0)
            self.q[eng].append(tk)
            vals.append((nm, v))
        dma_waits = [(nm, c) for nm, c in self.cnt.items()
                     if nm not in ("pe", "dve", "act", "pool", "bar")]
        bv = self.run("sync", lambda e: e.nop(), inc=("bar", 1),
                      waits=vals + dma_waits)
        for eng in ("tensor", "vector", "scalar", "gpsimd"):
            self.run(eng, lambda e: e.nop(), waits=[bv], nops=0)
        self.epoch += 1
        return bv

    def ps(self, name, shape):
        return self.stk.enter_context(self.nc.psum_tensor(name, shape, F32))

    def bump(self, name, by=1):
        self.sem(name)
        self.cnt[name] += by
        return self.cnt[name]

    ENGSEM = {"vector": "dve", "scalar": "act", "gpsimd": "pool"}

    def run(self, eng, f, inc=None, waits=(), nops=1):
        if (inc is not None and inc[1] >= 16
                and self.sem_epoch.get(inc[0], -1) < self.epoch
                and self.cnt.get(inc[0], 0) > 0):
            waits = list(waits) + [(inc[0], self.cnt[inc[0]])]
        if inc is not None and inc[1] >= 16:
            self.sem_epoch[inc[0]] = self.epoch
        flat = []
        for w in waits:
            if w is None:
                continue
            if isinstance(w, list):
                flat.extend(x for x in w if x is not None)
            else:
                flat.append(w)
        wl = tuple(w for w in flat if w[1])
        for nm_, v_ in wl:
            assert isinstance(nm_, str) and isinstance(v_, int), (
                f"bad wait {nm_!r} {v_!r}")
        esem = B.ENGSEM.get(eng)
        ret = None
        attach = None
        if esem is not None:
            n_data = nops if (inc is None or inc[0] == esem) else 0
            val = self.bump(esem, n_data) if n_data else self.cnt.get(esem, 0)
            if inc is not None:
                if inc[0] == esem:
                    ret = (esem, val)
                else:
                    attach = inc
                    ret = (inc[0], self.bump(*inc))
        else:
            if inc is not None:
                attach = inc
                ret = (inc[0], self.bump(*inc))
        for nm, _ in wl:
            self.sem(nm)
        sems = self.sems

        exp_after = self.cnt.get(esem, 0) if esem else None

        def thunk(e, f=f, attach=attach, wl=wl, exp=exp_after):
            for nm, v in wl:
                e.wait_ge(sems[nm], v)
            r = f(e)
            if attach is not None:
                r.then_inc(sems[attach[0]], attach[1])
            if exp is not None and hasattr(e, "_n") and e._n != exp:
                import traceback
                raise AssertionError(
                    f"nops drift: have {e._n} expected {exp} at thunk "
                    f"{getattr(f, '__code__', None) and f.__code__.co_filename}"
                    f":{f.__code__.co_firstlineno}")
        self.q[eng].append(thunk)
        return ret

    SELF_OPS = frozenset((
        "memset", "memzero", "tensor_copy", "tensor_tensor", "tensor_add",
        "tensor_sub", "tensor_mul", "tensor_max", "tensor_scalar",
        "tensor_single_scalar", "tensor_scalar_add", "tensor_scalar_sub",
        "tensor_scalar_mul", "tensor_scalar_max", "tensor_scalar_min",
        "tensor_relu", "tensor_reduce", "reduce_max", "reduce_sum",
        "scalar_tensor_tensor", "reciprocal", "select", "bn_stats",
        "bn_aggr", "pool", "pool_avg", "pool_max", "activation", "square",
        "sqrt", "mul", "copy", "add", "sign", "iota", "affine_select",
        "stream_shuffle"))

    class _SelfSync:
        def __init__(self, eng, sem):
            self._e = eng
            self._sem = sem
            self._n = 0

        def __getattr__(self, nm):
            real = getattr(self._e, nm)
            if nm in B.SELF_OPS:
                def call(*a, **k):
                    if self._n:
                        self._e.wait_ge(self._sem, self._n)
                    r = real(*a, **k)
                    r.then_inc(self._sem, 1)
                    self._n += 1
                    return r
                return call
            return real

    def finish(self, final_waits):
        for eng, nm, v in final_waits:
            self.sem(nm)
            sems = self.sems
            self.q[eng].append(lambda e, nm=nm, v=v: e.wait_ge(sems[nm], v))
        qq = self.q
        ss_dve = self.sem("dve")
        ss_act = self.sem("act")
        ss_pool = self.sem("pool")
        with self.nc.Block() as block:
            @block.tensor
            def _(t):
                for f in qq["tensor"]:
                    f(t)

            self._proxies = {}

            @block.vector
            def _(v):
                p = B._SelfSync(v, ss_dve)
                self._proxies["dve"] = p
                for f in qq["vector"]:
                    f(p)

            @block.scalar
            def _(s):
                p = B._SelfSync(s, ss_act)
                self._proxies["act"] = p
                for f in qq["scalar"]:
                    f(p)

            @block.gpsimd
            def _(g):
                p = B._SelfSync(g, ss_pool)
                self._proxies["pool"] = p
                for f in qq["gpsimd"]:
                    f(p)

            @block.sync
            def _(sy):
                for f in qq["sync"]:
                    f(sy)
        for nm, p in self._proxies.items():
            assert p._n == self.cnt.get(nm, 0), (
                f"self-sync drift on {nm}: emitted {p._n} ops, "
                f"build-time count {self.cnt.get(nm, 0)}")
        self.stk.close()


def make_consts():
    c = {}
    bf = ml_dtypes.bfloat16
    c["c_ident"] = np.eye(128, dtype=np.float32).astype(bf)
    c["c_ones"] = np.ones((1, 512), np.float32).astype(bf)
    i = np.arange(128)[:, None]
    j = np.arange(128)[None, :]
    c["c_maskA_T"] = np.where(j < i, NEG, 0.0).T.astype(np.float32).astype(bf)
    c["c_maskBo_T"] = np.where(j >= i, NEG, 0.0).T.astype(np.float32).astype(bf)
    c["c_maskBi_T"] = np.where(j > i, NEG, 0.0).T.astype(np.float32).astype(bf)
    t = np.arange(S)
    l = np.arange(64)
    lat = np.where(l[:, None] >= (t[None, :] // 64), NEG, 0.0).astype(np.float32)
    c["c_latmaskT"] = np.concatenate([lat, lat], 0).astype(bf)
    DH, W = 64, 512
    half = np.arange(0, DH, 2, dtype=np.float64)
    inv_freq = 1.0 / 10000.0 ** (half / DH)
    tp = np.arange(S, dtype=np.float64)
    fr = tp[:, None] * inv_freq[None, :]
    fr = np.concatenate([fr, fr], -1)
    base = (half + 0.4 * DH) / (1.4 * DH)
    power = (tp - S // 2) / (W // 2)
    xsc = base[None, :] ** power[:, None]
    xsc = np.concatenate([xsc, xsc], -1)
    sgn = np.where(np.arange(DH) < 32, -1.0, 1.0)[None, :]
    c["c_qA"] = (np.cos(fr) * xsc).astype(np.float32)
    c["c_qB"] = (sgn * np.sin(fr) * xsc).astype(np.float32)
    c["c_kA"] = (np.cos(fr) / xsc).astype(np.float32)
    c["c_kB"] = (sgn * np.sin(fr) / xsc).astype(np.float32)
    return c


WSHAPES = {
    "L0_attn_qkv_w": (D, 3 * D), "L0_attn_qkv_b": (3 * D,),
    "L0_attn_rq_w": (D, 256), "L0_attn_rq_b": (256,),
    "L0_attn_rkv_w": (D, 512), "L0_attn_rkv_b": (512,),
    "L0_attn_rout_w": (256, D), "L0_attn_rout_b": (D,),
    "L0_attn_out_w": (D, D), "L0_attn_out_b": (D,),
    "L0_attn_gl_w": (64, 1), "L0_attn_gr_w": (64, 1),
    "L0_ffn_wqkv": (2, D, 1536), "L0_ffn_q_scale": (2, 64),
    "L0_ffn_k_scale": (2, 64), "L0_ffn_wo": (2, 512, D),
    "L0_ffn_ln_g": (2, D), "L0_ffn_ln_b": (2, D),
    "L0_ffn_w1": (2, D, 5460), "L0_ffn_b1": (2, 5460),
    "L0_ffn_w2": (2, HID, D), "L0_ffn_b2": (2, D),
}
CONST_SHAPES = {
    "c_ident": (128, 128), "c_ones": (1, 512), "c_maskA_T": (128, 128),
    "c_maskBo_T": (128, 128), "c_maskBi_T": (128, 128),
    "c_latmaskT": (128, S), "c_qA": (S, 64), "c_qB": (S, 64),
    "c_kA": (S, 64), "c_kB": (S, 64),
}
CONST_DT = {k: BF16 for k in CONST_SHAPES}
for _n in ("c_qA", "c_qB", "c_kA", "c_kB"):
    CONST_DT[_n] = F32

DEBUG_SHAPES = {
    "ln1": ((S, D), BF16), "pooled": ((64, D), F32), "q": ((D, S), BF16),
    "k": ((D, S), BF16), "rq": ((256, S), BF16), "v": ((S, D), BF16),
    "localn": ((S, D), BF16), "latn": ((S, 256), BF16),
    "remote": ((S, D), BF16), "gate": ((S, 16), BF16),
    "mixed": ((S, D), BF16), "x1": ((S, D), F32), "z0": ((S, D), F32),
    "qk0": ((S, D), BF16), "qkr0": ((S, D), BF16), "vi0": ((S, 512), BF16),
    "o0": ((S, 512), BF16), "zwo0": ((S, D), F32), "y0": ((S, D), BF16),
    "hg0": ((128 * 22, S), BF16), "z1": ((S, D), F32),
    "yf1": ((1024, S), BF16),
    "qk1": ((S, D), BF16), "qkr1": ((S, D), BF16), "vi1": ((S, 512), BF16),
    "o1": ((S, 512), BF16), "zwo1": ((S, D), F32), "y1": ((S, D), BF16),
    "hg1": ((128 * 22, S), BF16), "z2": ((S, D), F32),
    "yf0": ((1024, S), BF16), "ys0": ((1024, S), BF16),
    "grep0": ((128, 1024), BF16), "brep0": ((128, 1024), BF16),
    "yfm0": ((128, 1024), BF16),
    "sc00": ((128, 1024), F32), "pr00": ((128, 512), BF16),
    "pT00": ((128, 512), BF16), "pv00": ((128, 1024), F32),
}


class ST(dict):
    pass


PHASE_A_FREE = [
    "wqk", "wv", "wrkv", "wrout", "wout", "bias_a", "bias_b",
    "glr", "grr", "rbrep", "obrep", "latmT", "maskBo", "xr", "ln1tm", "ln1fm",
    "pooledf", "pooledb", "qfm", "kfm", "rqfm", "vtm", "rkf", "rvf", "rvt",
    "probs", "probsT", "lprobs", "lprobsT", "nmx", "sexp", "RC", "RCL",
    "localn", "latn", "latfm", "remo", "lgl", "lgr", "gsig",
    "dtile", "mixfm",
]


def build(debug=(), phases=("A", "I", "F")):
    nc = bass.Bass("TRN2", target_bir_lowering=False)
    b = B(nc)
    H = {}
    H["L0_attn_x"] = nc.dram_tensor("L0_attn_x", [S, D], F32,
                                    kind="ExternalInput").ap()
    for n, sh in WSHAPES.items():
        H[n] = nc.dram_tensor(n, list(sh), F32, kind="ExternalInput").ap()
    for n, sh in CONST_SHAPES.items():
        H[n] = nc.dram_tensor(n, list(sh), CONST_DT[n], kind="ExternalInput").ap()
    H["out"] = nc.dram_tensor("out", [S, D], F32, kind="ExternalOutput").ap()
    H["x1s"] = nc.dram_tensor("x1s", [S, D], F32).ap()
    for zi in range(5):
        H[f"zs{zi}"] = nc.dram_tensor(f"zs{zi}", [S + 1, D], F32).ap()
    H["hs"] = nc.dram_tensor("hs", [22 * 128, S], BF16).ap()
    H["ysf"] = nc.dram_tensor("ysf", [8 * 128, S], BF16).ap()
    dbg = {}
    for name in debug:
        shp, dt = DEBUG_SHAPES[name]
        dbg[name] = nc.dram_tensor("dbg_" + name, list(shp), dt,
                                   kind="ExternalOutput").ap()
    st = ST()
    st["H"] = H
    st["dbg"] = dbg
    st["debug"] = set(debug)
    b.flush = b.sb("flushbuf", [128, 32], F32)
    b.run("gpsimd", lambda e: e.memset(b.flush[:], 0.0))
    emit_phase_A(b, st)
    b.barrier("A")
    b.free(*PHASE_A_FREE)
    if "I" in phases:
        for layer in range(2):
            emit_inner(b, st, layer)
    if "F" in phases:
        emit_final(b, st)
    b.finish([("sync", "st", b.cnt.get("st", 0))])
    return nc


def ln_chain(b, st, xap, sl, trash, W=1024, waits=(), guard=None):
    b.run("vector", lambda e: e.reduce_sum(sl[:, 0:1], xap, axis=AX.X,
                                           negate=True),
          waits=list(waits) + [guard])
    vd = b.run("vector", lambda e: e.tensor_scalar_mul(sl[:, 1:2], sl[:, 0:1],
                                                       1.0 / W), inc=("dve", 1))
    va = b.run("scalar", lambda e: e.activation(trash[:, 0:W], xap, AF.Square,
                                                bias=sl[:, 1:2], scale=1.0,
                                                accum_out=sl[:, 2:3]),
               inc=("act", 1), waits=[vd])
    vd2 = b.run("vector", lambda e: e.tensor_scalar(sl[:, 3:4], sl[:, 2:3],
                                                    1.0 / W, 1e-5, ALU.mult,
                                                    ALU.add),
                inc=("dve", 1), waits=[va])
    va2 = b.run("scalar", lambda e: e.activation(sl[:, 4:5], sl[:, 3:4], AF.Sqrt),
                inc=("act", 1), waits=[vd2])
    b.run("vector", lambda e: e.reciprocal(sl[:, 5:6], sl[:, 4:5]),
          waits=[va2])
    return b.run("vector", lambda e: e.tensor_mul(sl[:, 6:7], sl[:, 1:2],
                                                  sl[:, 5:6]), inc=("dve", 1))


def emit_phase_A(b, st):
    H = st["H"]
    dbg = st["dbg"]
    debug = st["debug"]

    ident = b.sb("ident", [128, 128], BF16)
    ones = b.sb("ones", [1, 512], BF16)
    maskA = b.sb("maskA", [128, 128], BF16)
    maskBi = b.sb("maskBi", [128, 128], BF16)
    stt = b.sb("stt", [128, 6, 8], F32)
    trash = b.sb("trash", [128, 1024], BF16)
    t1f = b.sb("t1f", [128, 1024], F32)
    t2f = b.sb("t2f", [128, 1024], F32)
    x1t = b.sb("x1t", [128, 1024], F32)
    maskBo = b.sb("maskBo", [128, 128], BF16)
    latmT = b.sb("latmT", [128, 2, 128], BF16)
    st["ident"], st["ones"], st["maskA"], st["maskBi"] = ident, ones, maskA, maskBi
    for sbuf, nm in ((ident, "c_ident"), (ones, "c_ones"), (maskA, "c_maskA_T"),
                     (maskBo, "c_maskBo_T"), (maskBi, "c_maskBi_T")):
        c_ld0 = b.run("sync", lambda e, s=sbuf, n=nm: e.dma_start(out=s[:],
                                                                  in_=H[n]),
                      inc=("cld", 16))
    st["c_consts"] = c_ld0

    wqk = b.sb("wqk", [128, 8, 2304], BF16)
    wv = b.sb("wv", [128, 8, 1024], BF16)
    wrkv = b.sb("wrkv", [128, 8, 512], BF16)
    wrout = b.sb("wrout", [128, 2, 1024], BF16)
    wout = b.sb("wout", [128, 8, 1024], BF16)
    bias_a = b.sb("bias_a", [1, 3328], BF16)
    bias_b = b.sb("bias_b", [1, 512], BF16)
    glr = b.sb("glr", [128, 1024], BF16)
    grr = b.sb("grr", [128, 1024], BF16)
    rbrep = b.sb("rbrep", [128, 1024], BF16)
    obrep = b.sb("obrep", [128, 1024], BF16)

    W_CUT = int(os.environ.get("W_CUT", "9"))
    qkvw = H["L0_attn_qkv_w"]
    for d in range(8 if W_CUT >= 2 else 0):
        b.run("gpsimd", lambda e, d=d: e.dma_start(
            out=wqk[:, d, 0:2048], in_=qkvw[d * 128:(d + 1) * 128, 0:2048]),
            inc=("wtA", 16))
        b.run("gpsimd", lambda e, d=d: e.dma_start(
            out=wqk[:, d, 2048:2304],
            in_=H["L0_attn_rq_w"][d * 128:(d + 1) * 128, :]), inc=("wtA", 16))
        b.run("gpsimd", lambda e, d=d: e.dma_start(
            out=wv[:, d, :], in_=qkvw[d * 128:(d + 1) * 128, 2048:3072]),
            inc=("wtA", 16))
        b.run("gpsimd", lambda e, d=d: e.dma_start(
            out=wrkv[:, d, :], in_=H["L0_attn_rkv_w"][d * 128:(d + 1) * 128, :]),
            inc=("wtA", 16))
        b.run("gpsimd", lambda e, d=d: e.dma_start(
            out=wout[:, d, :], in_=H["L0_attn_out_w"][d * 128:(d + 1) * 128, :]),
            inc=("wtA", 16))
    for c in range(2 if W_CUT >= 2 else 0):
        b.run("gpsimd", lambda e, c=c: e.dma_start(
            out=wrout[:, c, :], in_=H["L0_attn_rout_w"][c * 128:(c + 1) * 128, :]),
            inc=("wtA", 16))
    if W_CUT >= 3:
      b.run("gpsimd", lambda e: e.dma_start(out=bias_a[:, 0:3072],
                                            in_=H["L0_attn_qkv_b"][None, :]),
            inc=("wtA", 16))
      b.run("gpsimd", lambda e: e.dma_start(out=bias_a[:, 3072:3328],
                                            in_=H["L0_attn_rq_b"][None, :]),
            inc=("wtA", 16))
      b.run("gpsimd", lambda e: e.dma_start(out=bias_b[:, 0:512],
                                            in_=H["L0_attn_rkv_b"][None, :]),
            inc=("wtA", 16))

    if W_CUT < 4:
        c_wtA = ("wtA", b.cnt.get("wtA", 0))
        c_glr = None
        st["c_glr"] = None
    else:
      b.run("gpsimd", lambda e: e.dma_start(
          out=glr[:].rearrange("p (h j) -> p h j", j=64),
          in_=H["L0_attn_gl_w"].rearrange("a b -> (a b)")[None, None, :]
          .broadcast_to([128, 16, 64])), inc=("wtA", 16))
      b.run("gpsimd", lambda e: e.dma_start(
          out=grr[:].rearrange("p (h j) -> p h j", j=64),
          in_=H["L0_attn_gr_w"].rearrange("a b -> (a b)")[None, None, :]
          .broadcast_to([128, 16, 64])), inc=("wtA", 16))
      b.run("gpsimd", lambda e: e.dma_start(
          out=rbrep[:], in_=H["L0_attn_rout_b"][None, :]
          .broadcast_to([128, 1024])), inc=("wtA", 16))
      c_wtA = b.run("gpsimd", lambda e: e.dma_start(
          out=obrep[:], in_=H["L0_attn_out_b"][None, :]
          .broadcast_to([128, 1024])), inc=("wtA", 16))
      c_glr = c_wtA

      psA = b.ps("psA", [128, 1024])
      psB = b.ps("psB", [128, 1024])
      psPV = b.ps("psPV", [128, 1024])
      psL = b.ps("psL", [128, 1024])
      psLb = psL.bitcast(BF16)
      st["psA"], st["psB"], st["psPV"], st["psL"] = psA, psB, psPV, psL
      st["c_glr"] = c_glr

    xr = b.sb("xr", [128, 1, 1024], F32)
    ln1tm = b.sb("ln1tm", [128, 1, 1024], BF16)
    ln1fm = b.sb("ln1fm", [128, 2, 8, 512], BF16)
    pooledf = b.sb("pooledf", [128, 8, 64], F32)
    pooledb = b.sb("pooledb", [128, 8, 64], BF16)
    qfm = b.sb("qfm", [128, 2, 8, 512], BF16)
    kfm = b.sb("kfm", [128, 2, 8, 512], BF16)
    rqfm = b.sb("rqfm", [128, 2, 2, 512], BF16)
    vtm = b.sb("vtm", [128, 2, 4, 1024], BF16)
    rkf = b.sb("rkf", [128, 2, 64], BF16)
    rvf = b.sb("rvf", [128, 2, 64], BF16)
    rvt = b.sb("rvt", [128, 256], BF16)
    probs = b.sb("probs", [128, 2, 512], BF16)
    probsT = b.sb("probsT", [128, 2, 4, 128], BF16)
    lprobs = b.sb("lprobs", [128, 2, 128], BF16)
    lprobsT = b.sb("lprobsT", [128, 2, 128], BF16)
    nmx = b.sb("nmx", [128, 16], F32)
    sexp = b.sb("sexp", [128, 16], F32)
    RC = b.sb("RC", [128, 16], F32)
    RCL = b.sb("RCL", [128, 4], F32)
    localn = b.sb("localn", [128, 2, 1024], BF16)
    latn = b.sb("latn", [128, 2, 256], BF16)
    latfm = b.sb("latfm", [128, 2, 128], BF16)
    remo = b.sb("remo", [128, 1024], BF16)
    lgl = b.sb("lgl", [128, 16], F32)
    lgr = b.sb("lgr", [128, 16], F32)
    gsig = b.sb("gsig", [128, 16], BF16)
    dtile = b.sb("dtile", [128, 1024], BF16)
    mixfm = b.sb("mixfm", [128, 8, 128], BF16)

    st["stt"], st["trash"], st["x1t"], st["t1f"] = stt, trash, x1t, t1f
    A_CUT = int(os.environ.get("A_CUT", "10"))
    b.run("vector", lambda e: e.memset(probs[:], 0.0))
    b.run("vector", lambda e: e.memset(lprobs[:], 0.0))
    b.run("vector", lambda e: e.memset(rkf[:], 0.0))
    b.run("vector", lambda e: e.memset(rvf[:], 0.0))
    b.run("vector", lambda e: e.memset(pooledf[:], 0.0))
    vz = b.run("vector", lambda e: e.memset(t1f[:], 0.0), inc=("dve", 1))
    for zn in ("zs1", "zs3"):
        vzg = b.run("sync", lambda e, zn=zn: e.dma_start(out=H[zn][0:1, :],
                                                         in_=t1f[0:1, :]),
              inc=("stz0", 16), waits=[vz])
    st["g_t1f"] = vzg
    xin = H["L0_attn_x"]
    c_wt_full = b.cnt["wtA"]

    g_xr = [None, None]
    g_ln1tm = [None, None]
    ring_guard = [None, None]
    psg = {"psA": None, "psB": None, "psPV": None, "psL": None}
    g_probs = [None, None]
    g_probsT = [None, None]
    g_stA0 = [None, None]
    g_stPL = [None, None]
    g_stL2 = [None, None]
    qkv_evs = {}
    c_tr_sub = {}

    a1v_log = {}
    vpl_log = {}
    pool_log = {}

    def a0_block(i):
        r = i % 2
        for sub in range(4):
            g = 4 * i + sub
            slot = g % 2
            ts = 128 * g
            vld = b.run("sync", lambda e, ts=ts: e.dma_start(
                out=xr[:, 0, :], in_=xin[ts:ts + 128, :]),
                inc=("ldx0", 16), waits=[g_xr[0]])
            sl = stt[:, slot, :]
            w_rstd = ln_chain(b, st, xr[:, 0, :], sl, trash,
                              waits=[vld], guard=g_stA0[slot])
            va3 = b.run("scalar", lambda e, sl=sl: e.activation(
                ln1tm[:, 0, :], xr[:, 0, :], AF.Identity, bias=sl[:, 6:7],
                scale=sl[:, 5:6]), inc=("act", 1), waits=[w_rstd, g_ln1tm[0]])
            g_xr[0] = va3
            g_stA0[slot] = va3
            vtr = b.run("sync", lambda e, r=r, sub=sub: e.dma_start(
                out=ln1fm[:, r, :, sub * 128:(sub + 1) * 128],
                in_=ln1tm[:, 0, :], transpose=True), inc=(f"trl{r}", 16),
                waits=[va3, a1v_log.get(i - 2), vpl_log.get(i - 2)])
            g_ln1tm[0] = vtr
            c_tr_sub[g] = vtr
            if "ln1" in debug:
                vd_ln1 = b.run("scalar", lambda e, ts=ts: e.dma_start(
                    out=dbg["ln1"][ts:ts + 128, :], in_=ln1tm[:, 0, :]),
                    inc=("st", 16), waits=[vtr])
                g_ln1tm[0] = vd_ln1
        vpl = b.run("vector", lambda e, r=r, i=i: e.reduce_sum(
            pooledf[:, :, 8 * i:8 * (i + 1)].unsqueeze(3),
            ln1fm[:, r, :, :].rearrange("p c (a w) -> p c a w", w=64),
            axis=AX.X), inc=("dve", 1), waits=[c_tr_sub[4 * i + 3]])
        vplb = b.run("vector", lambda e, i=i: e.tensor_scalar_mul(
            pooledb[:, :, 8 * i:8 * (i + 1)], pooledf[:, :, 8 * i:8 * (i + 1)],
            1.0 / 64), inc=("dve", 1))
        vpl_log[i] = vpl
        pool_log[i] = vplb

    for i in range(NT5):
        if A_CUT < 2:
            break
        r = i % 2
        a0_block(i)
        vplb = pool_log[i]
        if A_CUT < 3:
            continue
        wt_w = ("wtA", c_wt_full) if i == 0 else None
        wt_c = st["c_consts"] if i == 0 else None
        ps_order = [psA[:, 0:512], psA[:, 512:1024], psB[:, 0:512],
                    psB[:, 512:1024]]
        names4 = ["psA", "psA", "psB", "psB"]
        slot_guard = [psg["psA"], psg["psA"], psg["psB"], psg["psB"]]
        evs = [None] * 18
        slot_guard_l = slot_guard

        def emit_a1_evac(fc, wait_v, pslot, dst, pap, scale):
            wv_ = [wait_v]
            if fc < 2:
                wv_.append(ring_guard[r])
            if fc % 2 == 0:
                if scale is None:
                    vev = b.run("vector", lambda e, dst=dst, pap=pap:
                                e.tensor_copy(dst, pap), inc=("dve", 1),
                                waits=wv_)
                else:
                    vev = b.run("vector", lambda e, dst=dst, pap=pap, s=scale:
                                e.tensor_scalar_mul(dst, pap, s),
                                inc=("dve", 1), waits=wv_)
            else:
                sc = 1.0 if scale is None else scale
                vev = b.run("scalar", lambda e, dst=dst, pap=pap, s=sc:
                            e.activation(dst, pap, AF.Copy, scale=s),
                            inc=("act", 1), waits=wv_)
            evs[fc] = vev
            slot_guard_l[pslot] = vev

        mm_log = []
        for fc in range(18):
            pslot = fc % 4
            pap = ps_order[pslot]
            fcol = fc * 128
            bcol = fcol if fc < 16 else 3072 + (fc - 16) * 128
            w = [wt_w, wt_c, c_tr_sub[4 * i + 3], slot_guard_l[pslot]]

            def mms(e, fcol=fcol, bcol=bcol, pap=pap, r=r):
                for d in range(8):
                    e.matmul(pap, wqk[:, d, fcol:fcol + 128], ln1fm[:, r, d, :],
                             start=(d == 0), stop=False)
                return e.matmul(pap, bias_a[0:1, bcol:bcol + 128], ones[0:1, :],
                                start=False, stop=True)
            vpe = b.run("tensor", mms, inc=("pe", 1), waits=w)
            if fc < 8:
                dst, scale = qfm[:, r, fc, :], 0.125
            elif fc < 16:
                dst, scale = kfm[:, r, fc - 8, :], None
            else:
                dst, scale = rqfm[:, r, fc - 16, :], 0.125
            mm_log.append((fc, vpe, pslot, dst, pap, scale))
            if fc >= 1:
                pf, pv, pp, pd, ppap, psc = mm_log[fc - 1]
                emit_a1_evac(pf, vpe, pp, pd, ppap, psc)
        psg["psA"] = [slot_guard[0], slot_guard[1]]
        psg["psB"] = [slot_guard[2], slot_guard[3]]
        qkv_evs[i] = evs
        v_log = []
        for sub in range(4):
            vps = psPV if sub % 2 == 0 else psL
            vkey = "psPV" if sub % 2 == 0 else "psL"
            w = [c_tr_sub[4 * i + sub], wt_w, psg[vkey]]
            if sub < 1:
                w.append(ring_guard[r])

            def vmm(e, r=r, sub=sub, vps=vps):
                for d in range(8):
                    e.matmul(vps[:, 0:512],
                             ln1fm[:, r, d, sub * 128:(sub + 1) * 128],
                             wv[:, d, 0:512], start=(d == 0), stop=False)
                    e.matmul(vps[:, 512:1024],
                             ln1fm[:, r, d, sub * 128:(sub + 1) * 128],
                             wv[:, d, 512:1024], start=(d == 0), stop=False)
                e.matmul(vps[:, 0:512], ones[0:1, 0:128], bias_a[0:1, 2048:2560],
                         start=False, stop=True)
                return e.matmul(vps[:, 512:1024], ones[0:1, 0:128],
                                bias_a[0:1, 2560:3072], start=False, stop=True)
            vpe = b.run("tensor", vmm, inc=("pe", 1), waits=w)
            if sub == 0:
                pf, pv, pp, pd, ppap, psc = mm_log[17]
                emit_a1_evac(pf, vpe, pp, pd, ppap, psc)
            else:
                pvpe, psub, pvps, pkey2 = v_log[-1]
                vev_ = b.run("vector", lambda e, r=r, psub=psub, pvps=pvps:
                             e.tensor_copy(vtm[:, r, psub, :], pvps[:]),
                             inc=("dve", 1), waits=[vpe])
                psg[pkey2] = vev_
            v_log.append((vpe, sub, vps, vkey))
        psg["psA"] = [slot_guard[0], slot_guard[1]]
        psg["psB"] = [slot_guard[2], slot_guard[3]]
        vfence0 = b.run("tensor", lambda e: e.matmul(
            psPV[:, 0:8], bias_a[0:1, 0:128], ones[0:1, 0:8], start=True,
            stop=True), inc=("pe", 1), waits=[psg["psPV"]])
        psg["psPV"] = vfence0
        pvpe, psub, pvps, pkey2 = v_log[-1]
        last_vev = b.run("vector", lambda e, r=r, psub=psub, pvps=pvps:
                         e.tensor_copy(vtm[:, r, psub, :], pvps[:]),
                         inc=("dve", 1), waits=[vfence0])
        psg[pkey2] = last_vev
        a1v_log[i] = v_log[-1][0]
        if "q" in debug:
            b.run("scalar", lambda e, r=r, i=i: e.dma_start(
                out=dbg["q"][:, i * 512:(i + 1) * 512]
                .rearrange("(c p) t -> p c t", p=128), in_=qfm[:, r, :, :]),
                inc=("st", 16), waits=evs[0:8])
        if "k" in debug:
            b.run("scalar", lambda e, r=r, i=i: e.dma_start(
                out=dbg["k"][:, i * 512:(i + 1) * 512]
                .rearrange("(c p) t -> p c t", p=128), in_=kfm[:, r, :, :]),
                inc=("st", 16), waits=evs[8:16])
        if "rq" in debug:
            b.run("scalar", lambda e, r=r, i=i: e.dma_start(
                out=dbg["rq"][:, i * 512:(i + 1) * 512]
                .rearrange("(c p) t -> p c t", p=128), in_=rqfm[:, r, :, :]),
                inc=("st", 16), waits=evs[16:18])
        if "v" in debug:
            b.run("scalar", lambda e, r=r, i=i: e.dma_start(
                out=dbg["v"][i * 512:(i + 1) * 512, :]
                .rearrange("(a p) d -> p a d", p=128), in_=vtm[:, r, :, :]),
                inc=("st", 16), waits=[last_vev])

        if A_CUT < 4:
            continue
        lat0 = 8 * i

        def rk_rv(e, lat0=lat0):
            last = None
            for c in range(2):
                pap = psL[:, c * 512:c * 512 + 8]
                for d in range(8):
                    e.matmul(pap, wrkv[:, d, c * 128:(c + 1) * 128],
                             pooledb[:, d, lat0:lat0 + 8], start=(d == 0),
                             stop=False)
                last = e.matmul(pap, bias_b[0:1, c * 128:(c + 1) * 128],
                                ones[0:1, 0:8], start=False, stop=True)
            for c in range(2):
                pap = psL[:, c * 512 + 256:c * 512 + 264]
                for d in range(8):
                    e.matmul(pap, wrkv[:, d, 256 + c * 128:256 + (c + 1) * 128],
                             pooledb[:, d, lat0:lat0 + 8], start=(d == 0),
                             stop=False)
                last = e.matmul(pap, bias_b[0:1, 256 + c * 128:256 + (c + 1) * 128],
                                ones[0:1, 0:8], start=False, stop=True)
            return last
        vpe = b.run("tensor", rk_rv, inc=("pe", 1),
                    waits=[vplb, wt_w, psg["psL"]])
        vfence = b.run("tensor", lambda e: e.matmul(
            psL[:, 8:16], bias_a[0:1, 0:128], ones[0:1, 0:8], start=True,
            stop=True), inc=("pe", 1))
        vev = b.run("vector", lambda e, i=i: (
            e.tensor_copy(rkf[:, 0, 8 * i:8 * i + 8], psL[:, 0:8]),
            e.tensor_copy(rkf[:, 1, 8 * i:8 * i + 8], psL[:, 512:520]),
            e.tensor_copy(rvf[:, 0, 8 * i:8 * i + 8], psL[:, 256:264]),
            e.tensor_copy(rvf[:, 1, 8 * i:8 * i + 8], psL[:, 768:776]))[-1],
            inc=("dve", 1), nops=4, waits=[vfence])

        def rvtr(e):
            e.transpose(psLb[0:64, 0:128], rvf[:, 0, :], ident[:])
            return e.transpose(psLb[0:64, 128:256], rvf[:, 1, :], ident[:])
        vpe2 = b.run("tensor", rvtr, inc=("pe", 1), waits=[vev])
        vrv = b.run("vector", lambda e: (
            e.tensor_copy(rvt[0:64, 0:128], psLb[0:64, 0:128]),
            e.tensor_copy(rvt[0:64, 128:256], psLb[0:64, 128:256]),
            e.tensor_copy(rvt[64:128, :], rvt[0:64, :]))[-1],
            inc=("dve", 1), nops=3, waits=[vpe2])
        psg["psL"] = vrv
        c_lat_ready = vrv

        if A_CUT < 5:
            continue
        last_pv = None
        def a2_headt(sub):
            g = 4 * i + sub
            ts = 128 * g
            lslot = g % 2
            pend_pv = None
            vex_last = None
            for hp in range(8):
                pspair = psA if hp % 2 == 0 else psB
                pkey = "psA" if hp % 2 == 0 else "psB"
                pslot = hp % 2
                kblocks = [g - 1, g] if g > 0 else [g]
                nb = len(kblocks)
                kaps = []
                for bk in kblocks:
                    ti = (bk // 4) % 2
                    kaps.append(kfm[:, ti, hp, (bk % 4) * 128:(bk % 4 + 1) * 128])
                w = [psg[pkey], evs[hp], evs[8 + hp]]
                if i > 0:
                    w.append(qkv_evs[i - 1][8 + hp])

                def smm(e, hp=hp, g=g, sub=sub, r=r, kaps=kaps, nb=nb,
                        pspair=pspair):
                    last = None
                    for he in range(2):
                        rows = slice(he * 64, (he + 1) * 64)
                        tp = (he * 64, 0)
                        base = he * 512
                        for jj, ka in enumerate(kaps):
                            e.matmul(pspair[:, base + jj * 128:base + (jj + 1) * 128],
                                     qfm[rows, r, hp, sub * 128:(sub + 1) * 128],
                                     ka[rows, :], start=(jj == 0), stop=False,
                                     tile_position=tp)
                        for jj in range(nb):
                            mk = maskBo if jj == nb - 1 else maskA
                            last = e.matmul(
                                pspair[:, base + jj * 128:base + (jj + 1) * 128],
                                mk[:], ident[:], start=False, stop=(jj == nb - 1))
                    return last
                vpe = b.run("tensor", smm, inc=("pe", 1), waits=w)
                if "sc00" in debug and g == 0 and hp == 0:
                    vdsc = b.run("vector", lambda e, pspair=pspair:
                                 e.tensor_copy(t1f[:], pspair[:]),
                                 inc=("dve", 1), waits=[vpe])
                    b.run("scalar", lambda e: e.dma_start(
                        out=dbg["sc00"], in_=t1f[:]), inc=("st", 16),
                        waits=[vdsc])
                vmx = b.run("vector", lambda e, hp=hp, pspair=pspair, nb=nb: (
                    e.reduce_max(nmx[:, 2 * hp:2 * hp + 1], pspair[:, 0:nb * 128],
                                 axis=AX.X, negate=True),
                    e.reduce_max(nmx[:, 2 * hp + 1:2 * hp + 2],
                                 pspair[:, 512:512 + nb * 128], axis=AX.X,
                                 negate=True))[-1], inc=("dve", 1),
                    nops=2, waits=[vpe, g_probs[pslot]])
                vex = b.run("scalar", lambda e, hp=hp, pslot=pslot, nb=nb,
                            pspair=pspair: (
                    e.activation(probs[:, pslot, 0:nb * 128],
                                 pspair[:, 0:nb * 128], AF.Exp,
                                 bias=nmx[:, 2 * hp:2 * hp + 1], scale=1.0,
                                 accum_out=sexp[:, 2 * hp:2 * hp + 1]),
                    e.activation(probs[:, pslot, 256:256 + nb * 128],
                                 pspair[:, 512:512 + nb * 128], AF.Exp,
                                 bias=nmx[:, 2 * hp + 1:2 * hp + 2], scale=1.0,
                                 accum_out=sexp[:, 2 * hp + 1:2 * hp + 2]))[-1],
                    inc=("act", 1), nops=2, waits=[vmx])
                psg[pkey] = vex
                vex_last = vex
                if "pr00" in debug and g == 0 and hp == 0:
                    b.run("scalar", lambda e, pslot=pslot: e.dma_start(
                        out=dbg["pr00"], in_=probs[:, pslot, :]),
                        inc=("st", 16), waits=[vex])
                vtp = b.run("sync", lambda e, pslot=pslot: e.dma_start(
                    out=probsT[:, pslot, :, :], in_=probs[:, pslot, :],
                    transpose=True), inc=(f"trp{pslot}", 16),
                    waits=[vex, g_probsT[pslot]])
                g_probs[pslot] = vtp

                def mk_pv(hp, pslot, vtp, kblocks=kblocks, nb=nb):
                    wpv = [vtp]
                    if hp == 0:
                        wpv += [psg["psPV"], st.get("g_pv2")]

                    def pvmm(e, hp=hp, pslot=pslot, kblocks=kblocks, nb=nb):
                        last = None
                        for he in range(2):
                            h = 2 * hp + he
                            for jj, bk in enumerate(kblocks):
                                ti = (bk // 4) % 2
                                vap = vtm[:, ti, bk % 4, 64 * h:64 * h + 64]
                                last = e.matmul(
                                    psPV[:, 64 * h:64 * h + 64],
                                    probsT[:, pslot, 2 * he + jj, :], vap,
                                    start=(h % 8 == 0 and jj == 0),
                                    stop=(h % 8 == 7 and jj == nb - 1))
                        return last
                    return pvmm, wpv
                if pend_pv is not None:
                    ppv, pw, ppslot = pend_pv
                    vpv = b.run("tensor", ppv, inc=("pe", 1), waits=pw)
                    g_probsT[ppslot] = vpv
                ppv, pw = mk_pv(hp, pslot, vtp)
                pend_pv = (ppv, pw, pslot)
            ppv, pw, ppslot = pend_pv
            vpv = b.run("tensor", ppv, inc=("pe", 1), waits=pw)
            g_probsT[ppslot] = vpv
            last_pv = vpv
            vrcA = b.run("vector", lambda e: e.reciprocal(RC[:, 0:16],
                                                          sexp[:, 0:16]),
                         inc=("dve", 1), waits=[vex_last])

            SKIP_LAT = os.environ.get("SKIP_LAT") == "1" or A_CUT < 6
            LAT_CUT = int(os.environ.get("LAT_CUT", "9"))
            Lg = 64
            mslot = g % 2
            if SKIP_LAT:
                vlm = None
            else:
                vlm = b.run("gpsimd", lambda e, g=g, mslot=mslot: e.dma_start(
                    out=latmT[:, mslot, :],
                    in_=H["c_latmaskT"][:, 128 * g:128 * (g + 1)]),
                    inc=(f"lmk{mslot}", 16), waits=[st.get(f"g_latm_{mslot}")])
            lat_ex = []
            for pg in range(2):
                if SKIP_LAT:
                    break
                w = [last_pv if pg == 0 else lat_ex[-1]]
                if pg == 0:
                    w += [c_lat_ready, psg["psL"]]

                def lsmm(e, pg=pg, g=g, sub=sub, r=r, Lg=Lg, mslot=mslot):
                    last = None
                    for he in range(2):
                        h = 2 * pg + he
                        rows = slice(he * 64, (he + 1) * 64)
                        tp = (he * 64, 0)
                        pap = psL[:, he * 512:he * 512 + Lg]
                        e.matmul(pap, rqfm[rows, r, pg, sub * 128:(sub + 1) * 128],
                                 rkf[rows, pg, 0:Lg], start=True, stop=False,
                                 tile_position=tp)
                        last = e.matmul(pap, latmT[rows, mslot, :],
                                        ident[rows, he * 64:he * 64 + Lg],
                                        start=False, stop=True, tile_position=tp)
                    return last
                vpe = b.run("tensor", lsmm, inc=("pe", 1),
                            waits=w + [vlm] if pg == 0 else w)
                if pg == 1:
                    st[f"g_latm_{mslot}"] = vpe
                vpef = b.run("tensor", lambda e: e.matmul(
                    psB[:, 8:16], bias_a[0:1, 0:128], ones[0:1, 0:8],
                    start=True, stop=True), inc=("pe", 1), waits=[psg["psB"]])
                psg["psB"] = vpef
                if LAT_CUT < 2:
                    lat_ex.append(vpef)
                    psg["psL"] = vpef
                    continue
                vmx = b.run("vector", lambda e, Lg=Lg: (
                    e.reduce_max(nmx[:, 0:1], psL[:, 0:Lg], axis=AX.X,
                                 negate=True),
                    e.reduce_max(nmx[:, 1:2], psL[:, 512:512 + Lg], axis=AX.X,
                                 negate=True))[-1], inc=("dve", 1),
                    nops=2, waits=[vpef])
                vex = b.run("scalar", lambda e, pg=pg, Lg=Lg: (
                    e.activation(lprobs[:, pg, 0:Lg], psL[:, 0:Lg], AF.Exp,
                                 bias=nmx[:, 0:1], scale=1.0,
                                 accum_out=sexp[:, 0:1]),
                    e.activation(lprobs[:, pg, 64:64 + Lg], psL[:, 512:512 + Lg],
                                 AF.Exp, bias=nmx[:, 1:2], scale=1.0,
                                 accum_out=sexp[:, 1:2]))[-1], inc=("act", 1),
                    nops=2, waits=[vmx, st.get("g_lprobs")])
                b.run("vector", lambda e, pg=pg: (
                    e.reciprocal(RCL[:, 2 * pg:2 * pg + 1], sexp[:, 0:1]),
                    e.reciprocal(RCL[:, 2 * pg + 1:2 * pg + 2],
                                 sexp[:, 1:2]))[-1], nops=2, waits=[vex])
                lat_ex.append(vex)
            if SKIP_LAT or LAT_CUT < 3:
                vtp = None
            elif True:
                vtp = b.run("sync", lambda e: e.dma_start(
                    out=lprobsT[:, :, :],
                    in_=lprobs[:, :, :].rearrange("p a c -> p (a c)"),
                    transpose=True), inc=("trlp", 16),
                    waits=[lat_ex[-1], st.get("g_lpT")])
                st["g_lprobs"] = vtp

            LPV_ON = (not SKIP_LAT) and LAT_CUT >= 4

            def lpv(e, Lg=Lg):
                last = None
                for h in range(4):
                    pg, he = h // 2, h % 2
                    rows = slice(he * 64, he * 64 + Lg)
                    col = 256 + 512 * (h % 2) + 64 * (h // 2)
                    last = e.matmul(psL[:, col:col + 64], lprobsT[rows, pg, :],
                                    rvt[rows, 64 * h:64 * h + 64],
                                    start=(h < 2), stop=(h >= 2))
                return last
            if SKIP_LAT or not LPV_ON:
                vpv = b.run("vector", lambda e: e.memset(latn[:], 0.0),
                            inc=("dve", 1), waits=[st.get("g_latn")])
            else:
                vpv = b.run("tensor", lpv, inc=("pe", 1),
                            waits=[vtp, lat_ex[-1]])
            st["g_lpT"] = vpv
            if SKIP_LAT or not LPV_ON or LAT_CUT == 4:
                vln = b.run("vector", lambda e: e.memset(latn[:], 0.0),
                            inc=("dve", 1),
                            waits=[st.get("g_latn"), vpv]) \
                    if LAT_CUT == 4 else vpv
            elif True:
                vpvf = b.run("tensor", lambda e: e.matmul(
                    psB[:, 8:16], bias_a[0:1, 0:128], ones[0:1, 0:8],
                    start=True, stop=True), inc=("pe", 1), waits=[psg["psB"]])
                psg["psB"] = vpvf
                latpv_view = psL[:].rearrange("p (x c) -> p x c", x=2)[:, :, 256:384] \
                    .rearrange("p x (h j) -> p x h j", j=64)
                vln = b.run("vector", lambda e, lv=latpv_view, lslot=lslot: e.tensor_tensor(
                    latn[:, lslot, :].rearrange("p (h a c) -> p a h c", a=2, c=64), lv,
                    RCL[:].rearrange("p (h a) -> p a h", a=2).unsqueeze(3)
                    .broadcast_to([128, 2, 2, 64]), ALU.mult),
                    inc=("dve", 1), waits=[vpvf, st.get(f"g_latn{lslot}")])
            if g == 0:
                vln = b.run("vector", lambda e, lslot=lslot: e.memset(
                    latn[0:64, lslot, :], 0.0), inc=("dve", 1))
            psg["psL"] = vln
            vloc = b.run("vector", lambda e, lslot=lslot: e.tensor_tensor(
                localn[:, lslot, :].rearrange("p (a c) -> p a c", c=64),
                psPV[:].rearrange("p (a c) -> p a c", c=64),
                RC[:].unsqueeze(2).broadcast_to([128, 16, 64]), ALU.mult),
                inc=("dve", 1), waits=[last_pv, st.get(f"g_localn{lslot}")])
            if g == 0:
                vloc = b.run("vector", lambda e, lslot=lslot: e.memset(
                    localn[0:1, lslot, :], 0.0), inc=("dve", 1))
            psg["psPV"] = vloc
            st["g_pv2"] = vloc
            if "localn" in debug:
                b.run("scalar", lambda e, ts=ts: e.dma_start(
                    out=dbg["localn"][ts:ts + 128, :], in_=localn[:, lslot, :]),
                    inc=("st", 16), waits=[vloc])
            if "latn" in debug:
                b.run("scalar", lambda e, ts=ts: e.dma_start(
                    out=dbg["latn"][ts:ts + 128, :], in_=latn[:, lslot, :]),
                    inc=("st", 16), waits=[vln])

            st["a_last_pv"] = last_pv
            return (sub, vln, vloc)

        def a2_tail2(sub, vln, vloc):
            g = 4 * i + sub
            ts = 128 * g
            lslot = g % 2
            if A_CUT < 7:
                return
            vtl = b.run("sync", lambda e, lslot=lslot: e.dma_start(
                out=latfm[:, :, :], in_=latn[:, lslot, :], transpose=True),
                inc=("trlf", 16), waits=[vln, st.get("g_latfm")])
            st[f"g_latn{lslot}"] = vtl

            def romm(e):
                last = None
                for c in range(2):
                    e.matmul(psA[:, 0:512], latfm[:, c, :], wrout[:, c, 0:512],
                             start=(c == 0), stop=(c == 1))
                    last = e.matmul(psA[:, 512:1024], latfm[:, c, :],
                                    wrout[:, c, 512:1024], start=(c == 0),
                                    stop=(c == 1))
                return last
            vro = b.run("tensor", romm, inc=("pe", 1),
                        waits=[vtl, psg["psA"]])
            st["g_latfm"] = vro
            vrof = b.run("tensor", lambda e: e.matmul(
                psB[:, 0:8], bias_a[0:1, 0:128], ones[0:1, 0:8], start=True,
                stop=True), inc=("pe", 1), waits=[psg["psB"]])
            psg["psB"] = vrof
            vrem = b.run("vector", lambda e: e.tensor_add(remo[:], psA[:],
                                                          rbrep[:]),
                         inc=("dve", 1), waits=[vrof, st.get("g_remo")])
            psg["psA"] = vrem

            if "remote" in debug:
                b.run("scalar", lambda e, ts=ts: e.dma_start(
                    out=dbg["remote"][ts:ts + 128, :], in_=remo[:]),
                    inc=("st", 16), waits=[vrem])

            if A_CUT < 8:
                return
            prod = dtile
            b.run("vector", lambda e, prod=prod, lslot=lslot: e.tensor_tensor(
                prod[:], localn[:, lslot, :], glr[:], ALU.mult),
                waits=([st["c_glr"]] if g == 0 and i == 0 else [])
                + [st.get("g_lgl"), st.get("g_x1t"), st.get("g_dtile")])
            b.run("vector", lambda e, prod=prod: e.reduce_sum(
                lgl[:].unsqueeze(2), prod[:].rearrange("p (a c) -> p a c", c=64),
                axis=AX.X))
            vp2 = b.run("vector", lambda e, prod=prod: e.tensor_tensor(
                prod[:], remo[:], grr[:], ALU.mult), waits=[vrem])
            b.run("vector", lambda e, prod=prod: e.reduce_sum(
                lgr[:].unsqueeze(2), prod[:].rearrange("p (a c) -> p a c", c=64),
                axis=AX.X))
            vga = b.run("vector", lambda e: e.tensor_add(lgl[:], lgl[:], lgr[:]),
                        inc=("dve", 1))
            vsg = b.run("scalar", lambda e: e.activation(gsig[:], lgl[:],
                                                         AF.Sigmoid),
                        inc=("act", 1), waits=[vga, st.get("g_gsig")])
            st["g_lgl"] = vsg
            if "gate" in debug:
                b.run("scalar", lambda e, ts=ts: e.dma_start(
                    out=dbg["gate"][ts:ts + 128, :], in_=gsig[:]),
                    inc=("st", 16), waits=[vsg])
            geng = "vector"
            gsem = "dve"
            vd_ = b.run(geng, lambda e, lslot=lslot: e.tensor_sub(
                dtile[:], localn[:, lslot, :], remo[:]),
                        inc=(gsem, 1), waits=[vsg])
            st[f"g_localn{lslot}"] = vd_
            b.run(geng, lambda e: e.tensor_tensor(
                dtile[:].rearrange("p (a c) -> p a c", c=64),
                dtile[:].rearrange("p (a c) -> p a c", c=64),
                gsig[:].unsqueeze(2).broadcast_to([128, 16, 64]), ALU.mult))
            vmix = b.run(geng, lambda e: e.tensor_add(dtile[:], dtile[:],
                                                      remo[:]),
                         inc=(gsem, 1))
            vmix = b.run(geng, lambda e: e.tensor_copy(
                b.flush[:, 24:32], dtile[:, 0:8]), inc=(gsem, 1))
            st["g_gsig"] = vmix
            if "mixed" in debug:
                b.run("scalar", lambda e, ts=ts: e.dma_start(
                    out=dbg["mixed"][ts:ts + 128, :], in_=dtile[:]),
                    inc=("st", 16), waits=[vmix])

            if A_CUT < 9:
                return
            vtm_ = b.run("sync", lambda e: e.dma_start(
                out=mixfm[:, :, :], in_=dtile[:], transpose=True),
                inc=("trmx", 16), waits=[vmix, st.get("g_mixfm")])
            st["g_dtile"] = vtm_
            st["g_remo"] = vmix
            vx0 = b.run("gpsimd", lambda e, ts=ts: e.dma_start(
                out=x1t[:], in_=xin[ts:ts + 128, :]), inc=("ldxr", 16),
                waits=[st.get("g_x1t")])

            def owmm(e):
                last = None
                for dch in range(8):
                    e.matmul(psB[:, 0:512], mixfm[:, dch, :], wout[:, dch, 0:512],
                             start=(dch == 0), stop=(dch == 7))
                    last = e.matmul(psB[:, 512:1024], mixfm[:, dch, :],
                                    wout[:, dch, 512:1024], start=(dch == 0),
                                    stop=(dch == 7))
                return last
            vow = b.run("tensor", owmm, inc=("pe", 1),
                        waits=[vtm_, psg["psB"]])
            st["g_mixfm"] = vow
            vowf = b.run("tensor", lambda e: e.matmul(
                psA[:, 0:8], bias_a[0:1, 0:128], ones[0:1, 0:8], start=True,
                stop=True), inc=("pe", 1), waits=[psg["psA"]])
            psg["psA"] = vowf
            vao = b.run("vector", lambda e: e.tensor_add(t2f[:], psB[:],
                                                         obrep[:]),
                        inc=("dve", 1), waits=[vowf, st.get("g_t2f")])
            psg["psB"] = vao
            slot = g % 2
            sl = stt[:, 2 + slot, :]
            w_rstd = ln_chain(b, st, t2f[:], sl, trash, waits=[vao],
                              guard=g_stPL[slot])
            vt1 = b.run("vector", lambda e, sl=sl: e.tensor_scalar(
                t1f[:], t2f[:], sl[:, 1:2], sl[:, 5:6], ALU.add, ALU.mult),
                inc=("dve", 1), waits=[w_rstd, vx0, st.get("g_t1f")])
            st["g_t2f"] = vt1
            g_stPL[slot] = vt1
            vx1 = b.run("vector", lambda e: e.tensor_add(x1t[:], x1t[:], t1f[:]),
                        inc=("dve", 1))
            vx1s = b.run("scalar", lambda e, ts=ts: e.dma_start(
                out=H["x1s"][ts:ts + 128, :], in_=x1t[:]), inc=("stx1", 16),
                waits=[vx1])
            if "x1" in debug:
                b.run("scalar", lambda e, ts=ts: e.dma_start(
                    out=dbg["x1"][ts:ts + 128, :], in_=x1t[:]), inc=("st", 16),
                    waits=[vx1])
            sl2 = stt[:, 4 + slot, :]
            w_rstd2 = ln_chain(b, st, x1t[:], sl2, trash, waits=[vx1],
                               guard=g_stL2[slot])
            va3 = b.run("scalar", lambda e, sl=sl2: e.activation(
                t1f[:], x1t[:], AF.Identity, bias=sl[:, 6:7], scale=sl[:, 5:6]),
                inc=("act", 1), waits=[w_rstd2])
            st["g_x1t"] = [va3, vx1s]
            g_stL2[slot] = va3
            vz0s = b.run("scalar", lambda e, ts=ts: e.dma_start(
                out=H["zs0"][1 + ts:129 + ts, :], in_=t1f[:]),
                inc=("stz0", 16), waits=[va3])
            st["g_t1f"] = vz0s
            if "z0" in debug:
                vz0d = b.run("scalar", lambda e, ts=ts: e.dma_start(
                    out=dbg["z0"][ts:ts + 128, :], in_=t1f[:]), inc=("st", 16),
                    waits=[va3])
                st["g_t1f"] = vz0d

        _a2p = []
        for sub in range(4):
            _h = a2_headt(sub)
            _a2p.append(_h)
            if len(_a2p) > 1:
                a2_tail2(*_a2p.pop(0))
        for _h in _a2p:
            a2_tail2(*_h)
        ring_guard[0] = st["a_last_pv"]
        ring_guard[1] = st["a_last_pv"]
        if "pooled" in debug and i == NT5 - 1:
            for cch in range(8):
                b.run("scalar", lambda e, cch=cch: e.dma_start(
                    out=dbg["pooled"].rearrange("l (c p) -> p c l", p=128)
                    [:, cch, :], in_=pooledf[:, cch, :],
                    allow_slow_non_contiguous=True),
                    inc=("st", 16), waits=[vplb])


def kernel(**inputs):
    consts = make_consts()
    nc = build()
    in_maps = []
    for c in range(8):
        m = {"L0_attn_x": np.ascontiguousarray(inputs["L0_attn_x"][c])}
        for n in WSHAPES:
            m[n] = np.ascontiguousarray(inputs[n], dtype=np.float32)
        m.update(consts)
        in_maps.append(m)
    res = run_bass_kernel_spmd(nc, in_maps, core_ids=list(range(8)))
    return np.stack([r["out"] for r in res.results]).astype(np.float32)


def emit_inner(b, st, l):
    H = st["H"]
    dbg = st["dbg"]
    debug = st["debug"]
    psA, psB, psPV, psL = st["psA"], st["psB"], st["psPV"], st["psL"]
    ident, ones = st["ident"], st["ones"]
    maskA, maskBi = st["maskA"], st["maskBi"]
    stt, trash, x1t, t1f = st["stt"], st["trash"], st["x1t"], st["t1f"]
    zin_name = f"zs{2 * l}"
    zmid_name = f"zs{2 * l + 1}"
    zout_name = f"zs{2 * l + 2}"
    L = l

    sfx = f"_{l}"
    b2row = b.sb("b2row" + sfx, [1, 1024], BF16)
    wqkv = b.sb("wqkv" + sfx, [128, 8, 1536], BF16)
    woi = b.sb("woi" + sfx, [128, 4, 1024], BF16)
    reps = b.sb("reps" + sfx, [128, 4, 64], BF16)
    grep = b.sb("grep" + sfx, [128, 1024], BF16)
    brep = b.sb("brep" + sfx, [128, 1024], BF16)
    for d in range(8):
        b.run("gpsimd", lambda e, d=d: e.dma_start(
            out=wqkv[:, d, :], in_=H["L0_ffn_wqkv"][L, d * 128:(d + 1) * 128, :]),
            inc=("wti", 16))
    for c in range(4):
        b.run("gpsimd", lambda e, c=c: e.dma_start(
            out=woi[:, c, :], in_=H["L0_ffn_wo"][L, c * 128:(c + 1) * 128, :]),
            inc=("wti", 16))
    c_wt0 = b.run("gpsimd", lambda e: e.dma_start(
        out=b2row[:], in_=H["L0_ffn_b2"][L][None, :]), inc=("wti", 16))

    qs = H["L0_ffn_q_scale"]
    ks = H["L0_ffn_k_scale"]
    b.run("gpsimd", lambda e: e.dma_start(
        out=reps[:, 0, :], in_=qs[L][None, :].broadcast_to([128, 64])),
        inc=("wti", 16))
    b.run("gpsimd", lambda e: e.dma_start(
        out=reps[:, 1, 0:32], in_=qs[L][None, 32:64].broadcast_to([128, 32])),
        inc=("wti", 16))
    b.run("gpsimd", lambda e: e.dma_start(
        out=reps[:, 1, 32:64], in_=qs[L][None, 0:32].broadcast_to([128, 32])),
        inc=("wti", 16))
    b.run("gpsimd", lambda e: e.dma_start(
        out=reps[:, 2, :], in_=ks[L][None, :].broadcast_to([128, 64])),
        inc=("wti", 16))
    b.run("gpsimd", lambda e: e.dma_start(
        out=reps[:, 3, 0:32], in_=ks[L][None, 32:64].broadcast_to([128, 32])),
        inc=("wti", 16))
    b.run("gpsimd", lambda e: e.dma_start(
        out=reps[:, 3, 32:64], in_=ks[L][None, 0:32].broadcast_to([128, 32])),
        inc=("wti", 16))
    b.run("gpsimd", lambda e: e.dma_start(
        out=grep[:], in_=H["L0_ffn_ln_g"][L][None, :].broadcast_to([128, 1024])),
        inc=("wti", 16))
    b.run("gpsimd", lambda e: e.dma_start(
        out=brep[:], in_=H["L0_ffn_ln_b"][L][None, :].broadcast_to([128, 1024])),
        inc=("wti", 16))
    c_grep = ("wti", b.cnt["wti"])
    c_reps = b.run("vector", lambda e: e.tensor_scalar_mul(
        reps[:, 0:2, :], reps[:, 0:2, :], 8.0), inc=("dve", 1),
        waits=[c_grep])

    zbf = b.sb("zbf" + sfx, [128, 4, 1024], BF16)
    zfm = b.sb("zfm" + sfx, [128, 2, 8, 528], BF16)
    qktm = b.sb("qktm" + sfx, [128, 2, 1024], BF16)
    tbl = b.sb("tbl" + sfx, [128, 2, 4, 64], BF16)
    ctb = b.sb("ctb" + sfx, [128, 2, 4, 64], BF16)
    tmp1 = b.sb("tmp1" + sfx, [128, 1024], BF16)
    tmp2 = b.sb("tmp2" + sfx, [128, 1024], BF16)
    qkrot = b.sb("qkrot" + sfx, [128, 2, 1024], BF16)
    qkfm = b.sb("qkfm" + sfx, [128, 2, 8, 512], BF16)
    vti = b.sb("vti" + sfx, [128, 2, 4, 512], BF16)
    iprobs = b.sb("iprobs" + sfx, [128, 2, 2, 640], BF16)
    iprobsT = b.sb("iprobsT" + sfx, [128, 2, 10, 128], BF16)
    inmx = b.sb("inmx" + sfx, [128, 8], F32)
    isexp = b.sb("isexp" + sfx, [128, 8], F32)
    RCi = b.sb("RCi" + sfx, [128, 8], F32)
    sqf = b.sb("sqf" + sfx, [128, 1024], F32)
    ssn = b.sb("ssn" + sfx, [128, 3, 16], F32)
    otm = b.sb("otm" + sfx, [128, 512], BF16)
    ofm = b.sb("ofm" + sfx, [128, 4, 128], BF16)
    zint = b.sb("zint" + sfx, [128, 2, 1024], F32)
    z1t = b.sb("z1t" + sfx, [128, 1024], F32)
    h2t = b.sb("h2t" + sfx, [128, 2, 1024], F32)
    zlast = b.sb("zlast" + sfx, [1, 512], F32)
    yt0 = b.sb("yt0" + sfx, [128, 1024], BF16)
    yfmt = b.sb("yfmt" + sfx, [128, 2, 8, 128], BF16)

    b.run("vector", lambda e: e.memset(iprobs[:], 0.0))
    g_zbf = [None, None, None, None]
    g_zfm = [None, None]
    ztr_log = {}
    g_qktm = [None, None]
    g_tbl = [None, None]
    g_qkrot = [None, None]
    ring_guard = [None, None]
    psg = {"psA": st.get("g_psA_inner"), "psB": None, "psPV": None, "psL":
           c_grep if False else None}
    g_iprobs = [None, None]
    g_iprobsT = [None, None]
    g_zint = [None, None]
    qk_evs = {}
    v_evs = {}
    g_stI = [None, None]

    zin = H[zin_name]
    zmid = H[zmid_name]
    zout = H[zout_name]
    ysf = H["ysf"]
    wt_all = ("wti", b.cnt["wti"])

    pv_log = {}
    smm_log = {}
    qkv_log = {}
    zgc_log = {}

    def sw_front(i, sub):
        r = i % 2
        rp = (i - 1) % 2
        g = 4 * i + sub
        ts = 128 * g
        slot = g % 2
        slot4 = g % 4
        vzb = b.run("gpsimd", lambda e, ts=ts, slot4=slot4: e.dma_start(
            out=zbf[:, slot4, :], in_=zin[1 + ts:129 + ts, :]),
            inc=(f"wtz{slot4}", 16), waits=[g_zbf[slot4]])
        vtr = b.run("sync", lambda e, r=r, sub=sub, slot4=slot4: e.dma_start(
            out=zfm[:, r, :, 16 + sub * 128:16 + (sub + 1) * 128],
            in_=zbf[:, slot4, :], transpose=True),
            inc=(f"trz{r}{sub}", 16),
            waits=[vzb, qkv_log.get(4 * (i - 2) + 3), zgc_log.get(i - 1)])
        g_zbf[slot4] = vtr
        ztr_log[(i, sub)] = vtr
        if sub == 0:
            if i == 0:
                vgc = b.run("vector", lambda e, r=r: e.memset(
                    zfm[:, r, :, 15:16], 0.0), inc=("dve", 1),
                    waits=[qkv_log.get(4 * (i - 2) + 3)])
            else:
                vgc = b.run("vector", lambda e, r=r, rp=rp: e.tensor_copy(
                    zfm[:, r, :, 15:16], zfm[:, rp, :, 527:528]),
                    inc=("dve", 1),
                    waits=[ztr_log[(i - 1, 3)],
                           qkv_log.get(4 * (i - 2) + 3)])
            g_zfm[r] = None
            st["_zgc"] = vgc
            zgc_log[i] = vgc
        w = [vtr, ztr_log.get((i, sub - 1)) if sub > 0 else None,
             st["_zgc"], wt_all if (i == 0 and sub == 0) else None,
             psg["psA"], psg["psB"]]

        def qkvmm(e, r=r, sub=sub):
            last = None
            for d in range(8):
                off = 16 + sub * 128 - (1 if d >= 4 else 0)
                lh = zfm[:, r, d, off:off + 128]
                e.matmul(psA[:, 0:512], lh, wqkv[:, d, 0:512],
                         start=(d == 0), stop=(d == 7))
                e.matmul(psA[:, 512:1024], lh, wqkv[:, d, 512:1024],
                         start=(d == 0), stop=(d == 7))
                last = e.matmul(psB[:, 0:512], lh, wqkv[:, d, 1024:1536],
                                start=(d == 0), stop=(d == 7))
            return last
        vpe = b.run("tensor", qkvmm, inc=("pe", 1), waits=w)
        qkv_log[g] = vpe
        vpef1 = b.run("tensor", lambda e: e.matmul(
            psPV[:, 0:8], ident[0:1, 0:128], ident[0:1, 0:8], start=True,
            stop=True), inc=("pe", 1), waits=[psg["psPV"]])
        psg["psPV"] = vpef1
        vqk = b.run("vector", lambda e, slot=slot: e.tensor_copy(
            qktm[:, slot, :], psA[:]), inc=("dve", 1),
            waits=[vpef1, g_qktm[slot]])
        psg["psA"] = vqk
        vvi = b.run("scalar", lambda e, r=r, sub=sub: e.activation(
            vti[:, r, sub, :], psB[:, 0:512], AF.Copy), inc=("act", 1),
            waits=[vpef1, pv_log.get(4 * i - 4 + sub)])
        psg["psB"] = vvi
        v_evs[(i, sub)] = vvi
        if f"qk{l}" in debug:
            b.run("scalar", lambda e, ts=ts, slot=slot: e.dma_start(
                out=dbg[f"qk{l}"][ts:ts + 128, :], in_=qktm[:, slot, :]),
                inc=("st", 16), waits=[vqk])
        if f"vi{l}" in debug:
            b.run("scalar", lambda e, ts=ts, r=r, sub=sub: e.dma_start(
                out=dbg[f"vi{l}"][ts:ts + 128, :], in_=vti[:, r, sub, :]),
                inc=("st", 16), waits=[vvi])

        for ci, cn in enumerate(("c_qA", "c_qB", "c_kA", "c_kB")):
            vtb = b.run("gpsimd", lambda e, ts=ts, slot=slot, ci=ci, cn=cn:
                        e.dma_start(out=tbl[:, slot, ci, :],
                                    in_=H[cn][ts:ts + 128, :]),
                        inc=(f"wtt{slot}", 16),
                        waits=[g_tbl[slot]] if ci == 0 else [])
        vct = b.run("vector", lambda e, slot=slot: e.tensor_tensor(
            ctb[:, slot, :, :], tbl[:, slot, :, :],
            reps[:, [0, 1, 2, 3], :] if False else reps[:, :, :], ALU.mult),
            inc=("dve", 1),
            waits=[vtb, c_reps if g == 0 else None])
        g_tbl[slot] = vct
        va = b.run("scalar", lambda e, slot=slot: e.activation(
            sqf[:], qktm[:, slot, :], AF.Square), inc=("act", 1),
            waits=[vqk, st.get("g_sqf")])
        b.run("vector", lambda e, slot=slot: e.tensor_tensor(
            tmp1[:, 0:512].rearrange("p (h c) -> p h c", c=64),
            qktm[:, slot, 0:512].rearrange("p (h c) -> p h c", c=64),
            ctb[:, slot, 0:1, :].broadcast_to([128, 8, 64]), ALU.mult),
            waits=[vct])
        b.run("vector", lambda e, slot=slot: e.tensor_tensor(
            tmp1[:, 512:1024].rearrange("p (h c) -> p h c", c=64),
            qktm[:, slot, 512:1024].rearrange("p (h c) -> p h c", c=64),
            ctb[:, slot, 2:3, :].broadcast_to([128, 8, 64]), ALU.mult))
        for base, tc in ((0, 1), (512, 3)):
            b.run("vector", lambda e, slot=slot, base=base, tc=tc: (
                e.tensor_tensor(
                    tmp2[:, base:base + 512]
                    .rearrange("p (h x) -> p h x", x=64)[:, :, 0:32],
                    qktm[:, slot, base + 32:base + 512 + 32]
                    .rearrange("p (h x) -> p h x", x=64)[:, :, 0:32]
                    if False else
                    qktm[:, slot, base:base + 512]
                    .rearrange("p (h x) -> p h x", x=64)[:, :, 32:64],
                    ctb[:, slot, tc:tc + 1, 0:32].broadcast_to([128, 8, 32]),
                    ALU.mult),
                e.tensor_tensor(
                    tmp2[:, base:base + 512]
                    .rearrange("p (h x) -> p h x", x=64)[:, :, 32:64],
                    qktm[:, slot, base:base + 512]
                    .rearrange("p (h x) -> p h x", x=64)[:, :, 0:32],
                    ctb[:, slot, tc:tc + 1, 32:64].broadcast_to([128, 8, 32]),
                    ALU.mult))[-1], nops=2)
        vt12 = b.run("vector", lambda e: e.tensor_add(tmp1[:], tmp1[:],
                                                      tmp2[:]),
                     inc=("dve", 1))
        vss = b.run("vector", lambda e, slot=slot: e.reduce_sum(
            ssn[:, 0, :].unsqueeze(2),
            sqf[:].rearrange("p (h c) -> p h c", c=64), axis=AX.X),
            inc=("dve", 1), waits=[va, g_stI[slot]])
        st["g_sqf"] = vss
        vsd = b.run("scalar", lambda e: e.activation(ssn[:, 1, :],
                                                     ssn[:, 0, :], AF.Sqrt),
                    inc=("act", 1), waits=[vss])
        b.run("vector", lambda e: e.tensor_scalar_max(ssn[:, 1, :],
                                                      ssn[:, 1, :], 1e-12),
              waits=[vsd])
        vrn = b.run("vector", lambda e: e.reciprocal(ssn[:, 2, :],
                                                     ssn[:, 1, :]),
                    inc=("dve", 1))
        vqr = b.run("vector", lambda e, slot=slot: e.tensor_tensor(
            qkrot[:, slot, :].rearrange("p (h c) -> p h c", c=64),
            tmp1[:].rearrange("p (h c) -> p h c", c=64),
            ssn[:, 2, :].unsqueeze(2).broadcast_to([128, 16, 64]), ALU.mult),
            inc=("dve", 1), waits=[vrn, g_qkrot[slot]])
        g_stI[slot] = vqr
        g_qktm[slot] = vqr
        if f"qkr{l}" in debug:
            b.run("scalar", lambda e, ts=ts, slot=slot: e.dma_start(
                out=dbg[f"qkr{l}"][ts:ts + 128, :], in_=qkrot[:, slot, :]),
                inc=("st", 16), waits=[vqr])
        vtr2 = b.run("sync", lambda e, r=r, sub=sub, slot=slot: e.dma_start(
            out=qkfm[:, r, :, sub * 128:(sub + 1) * 128],
            in_=qkrot[:, slot, :], transpose=True),
            inc=(f"trq{r}{sub}", 16),
            waits=[vqr, smm_log.get(4 * i - 4 + sub)])
        g_qkrot[slot] = vtr2
        qk_evs[(i, sub)] = vtr2


    def sw_back(i, sub):
        r = i % 2
        g = 4 * i + sub
        ts = 128 * g
        slot = g % 2

        kbl = [bk for bk in range(g - 4, g + 1) if bk >= 0]
        nb = len(kbl)
        pend_pv = None
        vex_last = None
        for q in range(4):
            pslot = q % 2
            w = [psg["psA"], psg["psB"], qk_evs[(i, sub)]]
            for bk in kbl:
                w.append(qk_evs[(bk // 4, bk % 4)])

            def smm(e, q=q, g=g, sub=sub, r=r, kbl=kbl, nb=nb):
                for jj, bk in enumerate(kbl):
                    sl_ = (bk // 4) % 2
                    for he, pspair in ((0, psA), (1, psB)):
                        rows = slice(he * 64, (he + 1) * 64)
                        ka = qkfm[rows, sl_, 4 + q,
                                  (bk % 4) * 128:(bk % 4 + 1) * 128]
                        e.matmul(pspair[:, jj * 128:(jj + 1) * 128],
                                 qkfm[rows, r, q, sub * 128:(sub + 1) * 128],
                                 ka, start=(jj == 0 or jj == 4), stop=False,
                                 tile_position=(he * 64, 0))
                for he, pspair in ((0, psA), (1, psB)):
                    if nb == 5:
                        e.matmul(pspair[:, 0:128], maskA[:], ident[:],
                                 start=False, stop=True)
                    e.matmul(pspair[:, (nb - 1) * 128:nb * 128],
                             maskBi[:], ident[:], start=False, stop=True)
                return e.matmul(psPV[:, 512:520], ident[0:1, 0:128],
                                ident[0:1, 0:8], start=True, stop=True)
            vpe = b.run("tensor", smm, inc=("pe", 1), waits=w)
            smm_log[g] = vpe
            vmx0 = b.run("vector", lambda e, q=q, nb=nb: e.reduce_max(
                inmx[:, 2 * q:2 * q + 1], psA[:, 0:nb * 128],
                axis=AX.X, negate=True),
                inc=("dve", 1), waits=[vpe, g_iprobs[pslot]])
            vmx1 = b.run("vector", lambda e, q=q, nb=nb: e.reduce_max(
                inmx[:, 2 * q + 1:2 * q + 2], psB[:, 0:nb * 128],
                axis=AX.X, negate=True), inc=("dve", 1))
            vex0 = b.run("scalar", lambda e, q=q, pslot=pslot, nb=nb:
                         e.activation(
                iprobs[:, pslot, 0, 0:nb * 128], psA[:, 0:nb * 128], AF.Exp,
                bias=inmx[:, 2 * q:2 * q + 1], scale=1.0,
                accum_out=isexp[:, 2 * q:2 * q + 1]), inc=("act", 1),
                waits=[vmx0])
            vex1 = b.run("scalar", lambda e, q=q, pslot=pslot, nb=nb:
                         e.activation(
                iprobs[:, pslot, 1, 0:nb * 128], psB[:, 0:nb * 128], AF.Exp,
                bias=inmx[:, 2 * q + 1:2 * q + 2], scale=1.0,
                accum_out=isexp[:, 2 * q + 1:2 * q + 2]), inc=("act", 1),
                waits=[vmx1])
            psg["psA"] = vex0
            psg["psB"] = vex1
            vex_last = vex1
            vtp = b.run("sync", lambda e, pslot=pslot: e.dma_start(
                out=iprobsT[:, pslot, :, :],
                in_=iprobs[:, pslot, :, :].rearrange("p a c -> p (a c)"),
                transpose=True), inc=(f"trip{pslot}", 16),
                waits=[vex1, g_iprobsT[pslot]])
            g_iprobs[pslot] = vtp

            def mk_pv(q, pslot, vtp, kbl=kbl, nb=nb):
                wpv = [vtp]
                if q == 0:
                    wpv += [psg["psPV"], v_evs[(i, sub)]]

                def pvmm(e, q=q, pslot=pslot, kbl=kbl, nb=nb):
                    last = None
                    for he in (0, 1):
                        h = 2 * q + he
                        for jj, bk in enumerate(kbl):
                            sl_ = (bk // 4) % 2
                            vap = vti[:, sl_, bk % 4, 64 * h:64 * h + 64]
                            last = e.matmul(
                                psPV[:, 64 * h:64 * h + 64],
                                iprobsT[:, pslot, he * 5 + jj, :], vap,
                                start=(h == 0 and jj == 0),
                                stop=(h == 7 and jj == nb - 1))
                    return last
                return pvmm, wpv
            if pend_pv is not None:
                ppv, pw, ppslot = pend_pv
                vpv = b.run("tensor", ppv, inc=("pe", 1), waits=pw)
                g_iprobsT[ppslot] = vpv
            ppv, pw = mk_pv(q, pslot, vtp)
            pend_pv = (ppv, pw, pslot)
        ppv, pw, ppslot = pend_pv
        vpv = b.run("tensor", ppv, inc=("pe", 1), waits=pw)
        g_iprobsT[ppslot] = vpv
        pv_log[g] = vpv
        last_pv = b.run("tensor", lambda e: e.matmul(
            psPV[:, 520:528], ident[0:1, 0:128], ident[0:1, 0:8],
            start=True, stop=True), inc=("pe", 1))
        vrc = b.run("vector", lambda e: e.reciprocal(RCi[:, 0:8],
                                                     isexp[:, 0:8]),
                    inc=("dve", 1), waits=[vex_last])
        voe = b.run("vector", lambda e: e.tensor_tensor(
            otm[:].rearrange("p (h c) -> p h c", c=64),
            psPV[:, 0:512].rearrange("p (h c) -> p h c", c=64),
            RCi[:].unsqueeze(2).broadcast_to([128, 8, 64]), ALU.mult),
            inc=("dve", 1), waits=[last_pv, st.get("g_otm")])
        psg["psPV"] = voe
        if f"o{l}" in debug:
            b.run("scalar", lambda e, ts=ts: e.dma_start(
                out=dbg[f"o{l}"][ts:ts + 128, :], in_=otm[:]),
                inc=("st", 16), waits=[voe])
        vto = b.run("sync", lambda e: e.dma_start(
            out=ofm[:, :, :], in_=otm[:], transpose=True),
            inc=("tro", 16), waits=[voe, st.get("g_ofm")])
        st["g_otm"] = vto

        vzl = b.run("sync", lambda e, ts=ts, slot=slot: e.dma_start(
            out=zint[:, slot, :], in_=zin[1 + ts:129 + ts, :]),
            inc=(f"ldz{slot}", 16), waits=[g_zint[slot]])

        def womm(e):
            last = None
            for c in range(4):
                e.matmul(psL[:, 0:512], ofm[:, c, :], woi[:, c, 0:512],
                         start=(c == 0), stop=(c == 3))
                last = e.matmul(psL[:, 512:1024], ofm[:, c, :],
                                woi[:, c, 512:1024],
                                start=(c == 0), stop=(c == 3))
            return last
        vwo = b.run("tensor", womm, inc=("pe", 1),
                    waits=[vto, psg["psL"]])
        st["g_ofm"] = vwo
        vwof = b.run("tensor", lambda e: e.matmul(
            psPV[:, 8:16], ident[0:1, 0:128], ident[0:1, 0:8], start=True,
            stop=True), inc=("pe", 1), waits=[psg["psPV"]])
        psg["psPV"] = vwof
        vz1 = b.run("vector", lambda e, slot=slot: e.tensor_add(
            z1t[:], psL[:], zint[:, slot, :]), inc=("dve", 1),
            waits=[vwof, vzl, st.get("g_z1t")])
        psg["psL"] = vz1
        g_zint[slot] = vz1
        vz1s = b.run("scalar", lambda e, ts=ts: e.dma_start(
            out=zmid[1 + ts:129 + ts, :], in_=z1t[:]),
            inc=(f"stm{slot}", 16), waits=[vz1])
        st["g_z1t"] = vz1s
        if f"zwo{l}" in debug:
            b.run("scalar", lambda e, ts=ts: e.dma_start(
                out=dbg[f"zwo{l}"][ts:ts + 128, :], in_=z1t[:]),
                inc=("st", 16), waits=[vz1])

        vh0 = b.run("vector", lambda e, slot=slot: e.tensor_copy(
            h2t[:, slot, 0:512], z1t[:, 0:512]), inc=("dve", 1),
            waits=[st.get(f"g_h2t{slot}")])
        vh1 = b.run("sync", lambda e, ts=ts, slot=slot: e.dma_start(
            out=h2t[:, slot, 512:1024], in_=zmid[ts:128 + ts, 512:1024]),
            inc=(f"ldh{slot}", 16),
            waits=[vz1s, st.get(f"g_stm{1 - slot}"), st.get(f"g_h2t{slot}")])
        st[f"g_stm{slot}"] = vz1s
        st["g_z1t"] = [st.get("g_z1t"), vh0]
        return (vh0, vh1)

    def sw_i5(i, sub, vh0, vh1):
        g = 4 * i + sub
        ts = 128 * g
        slot = g % 2
        sl = stt[:, slot, :]
        w_rstd = ln_chain(b, st, h2t[:, slot, :], sl, trash,
                          waits=[vh0, vh1],
                          guard=st.get(f"g_stI5_{slot}"))
        vya = b.run("scalar", lambda e, sl=sl, slot=slot: e.activation(
            yt0[:], h2t[:, slot, :], AF.Identity, bias=sl[:, 6:7],
            scale=sl[:, 5:6]),
            inc=("act", 1), waits=[w_rstd, st.get("g_yt0")])
        vya = b.run("scalar", lambda e, slot=slot: e.activation(
            b.flush[:, 16:24], h2t[:, slot, 0:8], AF.Copy), inc=("act", 1))
        st[f"g_stI5_{slot}"] = vya
        st[f"g_h2t{slot}"] = vya
        vyg = b.run("vector", lambda e: e.tensor_mul(yt0[:], yt0[:], grep[:]),
                    inc=("dve", 1),
                    waits=[vya,
                           c_grep if g == 0 else None])
        vyb = b.run("vector", lambda e: e.tensor_add(yt0[:], yt0[:], brep[:]),
                    inc=("dve", 1))
        vyb = b.run("vector", lambda e: e.tensor_copy(
            b.flush[:, 8:16], yt0[:, 0:8]), inc=("dve", 1))
        if f"y{l}" in debug:
            b.run("scalar", lambda e, ts=ts: e.dma_start(
                out=dbg[f"y{l}"][ts:ts + 128, :], in_=yt0[:]),
                inc=("st", 16), waits=[vyb])
        yslot = g % 2
        vty = b.run("scalar", lambda e, yslot=yslot: e.dma_start(
            out=yfmt[:, yslot, :, :], in_=yt0[:], transpose=True),
            inc=(f"try{yslot}", 16),
            waits=[vyb, st.get(f"g_yfmt{yslot}")])
        st["g_yt0"] = vty
        vys = b.run("scalar", lambda e, ts=ts, yslot=yslot: e.dma_start(
            out=ysf.rearrange("(c p) t -> p c t", p=128)[:, :, ts:ts + 128],
            in_=yfmt[:, yslot, :, :]), inc=(f"sty{yslot}", 16),
            waits=[vty])
        st[f"g_yfmt{yslot}"] = vys
        if sub == 3:
            st.setdefault(f"ydone{l}", {})[i] = [
                ("sty0", b.cnt.get("sty0", 0)),
                ("sty1", b.cnt.get("sty1", 0))]

    _pend = []
    _mids = []
    for i in range(NT5):
        for sub in range(4):
            sw_front(i, sub)
            _pend.append((i, sub))
            if len(_pend) > 2:
                mi, msub = _pend.pop(0)
                _mids.append((mi, msub) + sw_back(mi, msub))
            if len(_mids) > 1:
                sw_i5(*_mids.pop(0))
    while _pend:
        mi, msub = _pend.pop(0)
        _mids.append((mi, msub) + sw_back(mi, msub))
        if len(_mids) > 1:
            sw_i5(*_mids.pop(0))
    while _mids:
        sw_i5(*_mids.pop(0))

    c_y_done = [("sty0", b.cnt.get("sty0", 0)),
                ("sty1", b.cnt.get("sty1", 0))]
    if l == 0 and "grep0" in debug:
        b.run("scalar", lambda e: e.dma_start(
            out=dbg["grep0"], in_=grep[:]), inc=("st", 16), waits=c_y_done)
        b.run("scalar", lambda e: e.dma_start(
            out=dbg["brep0"], in_=brep[:]), inc=("st", 16), waits=c_y_done)
    if l == 0 and "yfm0" in debug:
        b.run("scalar", lambda e: e.dma_start(
            out=dbg["yfm0"].rearrange("p (c t) -> p c t", t=128),
            in_=yfmt[:, 1, :, :]), inc=("st", 16), waits=c_y_done)
    st["g_psA_inner"] = psg["psA"]
    b.barrier(f"sweep1_{l}")
    b.free("zbf" + sfx, "zfm" + sfx, "qktm" + sfx, "tbl" + sfx, "ctb" + sfx,
           "tmp1" + sfx, "tmp2" + sfx, "qkrot" + sfx, "qkfm" + sfx,
           "vti" + sfx, "iprobs" + sfx, "iprobsT" + sfx, "inmx" + sfx,
           "isexp" + sfx, "RCi" + sfx, "sqf" + sfx, "ssn" + sfx, "otm" + sfx,
           "ofm" + sfx, "zint" + sfx, "z1t" + sfx,
           "h2t" + sfx, "zlast" + sfx, "yt0" + sfx, "yfmt" + sfx,
           "reps" + sfx, "grep" + sfx, "brep" + sfx,
           "wqkv" + sfx, "woi" + sfx)

    w2sb = b.sb("w2sb" + sfx, [128, 22, 1024], BF16)
    w1full = b.sb("w1full" + sfx, [128, 8, 2 * HID], BF16)
    b1s = b.sb("b1s" + sfx, [1, 2, 2, 128], BF16)
    ggt = b.sb("ggt" + sfx, [128, 512], BF16)
    ytt = b.sb("ytt" + sfx, [128, 8, 512], BF16)
    htt = b.sb("htt" + sfx, [128, 22, 512], BF16)
    z2t = b.sb("z2t" + sfx, [128, 1024], F32)
    zmt = b.sb("zmt" + sfx, [128, 2, 1024], F32)
    for j in range(22):
        hj = min(128, HID - j * 128)
        b.run("gpsimd", lambda e, j=j, hj=hj: e.dma_start(
            out=w2sb[0:hj, j, :],
            in_=H["L0_ffn_w2"][L, j * 128:j * 128 + hj, :]), inc=("wtu", 16))
    c_w2 = ("wtu", b.cnt["wtu"])
    for d in range(8):
        b.run("gpsimd", lambda e, d=d: e.dma_start(
            out=w1full[:, d, :],
            in_=H["L0_ffn_w1"][L, d * 128:(d + 1) * 128, :]),
            inc=("wtw0", 16))
    c_w1 = ("wtw0", b.cnt["wtw0"])
    if l == 0 and "ys0" in debug:
        b.run("scalar", lambda e: e.dma_start(
            out=dbg["ys0"], in_=ysf[:, :]), inc=("st", 16), waits=c_y_done)
    g_ytt = None
    g_htt = None
    g_b1 = [None, None]
    g_zmt = [None, None]
    ydone = st[f"ydone{l}"]
    for tt in range(NT5):
        vyl = None
        for c in range(8):
            vyl = b.run("sync", lambda e, c=c, tt=tt: e.dma_start(
                out=ytt[:, c, :],
                in_=ysf[c * 128:(c + 1) * 128, tt * 512:(tt + 1) * 512]),
                inc=("ldy", 16),
                waits=(ydone[tt] + [g_ytt]) if c == 0 else [])
        vh = None
        for j in range(22):
            fw = min(128, HID - j * 128)
            slot = j % 2
            b.run("gpsimd", lambda e, j=j, fw=fw, slot=slot: e.dma_start(
                out=b1s[:, slot, 0, 0:fw],
                in_=H["L0_ffn_b1"][L][None, j * 128:j * 128 + fw]),
                inc=(f"wtb{slot}", 16), waits=[g_b1[slot]])
            vb1 = b.run("gpsimd", lambda e, j=j, fw=fw, slot=slot: e.dma_start(
                out=b1s[:, slot, 1, 0:fw],
                in_=H["L0_ffn_b1"][L][None, HID + j * 128:HID + j * 128 + fw]),
                inc=(f"wtb{slot}", 16))

            def w1mm(e, j=j, fw=fw, slot=slot, tt=tt):
                for d in range(8):
                    e.matmul(psA[0:fw, 0:512],
                             w1full[:, d, j * 128:j * 128 + fw],
                             ytt[:, d, :], start=(d == 0), stop=False)
                    e.matmul(psB[0:fw, 0:512],
                             w1full[:, d, HID + j * 128:HID + j * 128 + fw],
                             ytt[:, d, :], start=(d == 0), stop=False)
                e.matmul(psA[0:fw, 0:512], b1s[0:1, slot, 0, 0:fw],
                         ones[0:1, :], start=False, stop=True)
                return e.matmul(psB[0:fw, 0:512], b1s[0:1, slot, 1, 0:fw],
                                ones[0:1, :], start=False, stop=True)
            vpe = b.run("tensor", w1mm, inc=("pe", 1),
                        waits=[vb1, vyl if j == 0 else None,
                               c_w1 if (tt == 0 and j == 0) else None,
                               psg["psA"], psg["psB"]])
            g_b1[slot] = vpe
            vpef2 = b.run("tensor", lambda e: e.matmul(
                psPV[:, 0:8], ident[0:1, 0:128], ident[0:1, 0:8], start=True,
                stop=True), inc=("pe", 1))
            vgg = b.run("scalar", lambda e, fw=fw: e.activation(
                ggt[0:fw, :], psB[0:fw, 0:512], AF.Gelu), inc=("act", 1),
                waits=[vpef2])
            psg["psB"] = vgg
            vh = b.run("vector", lambda e, fw=fw, j=j: e.tensor_tensor(
                htt[0:fw, j, :], psA[0:fw, 0:512], ggt[0:fw, :], ALU.mult),
                inc=("dve", 1),
                waits=[vgg, g_htt if j == 0 else None])
            psg["psA"] = vh
        g_ytt = vpe
        for sub in range(4):
            g = 4 * tt + sub
            ts = 128 * g
            slot = g % 2
            vzl = b.run("sync", lambda e, ts=ts, slot=slot: e.dma_start(
                out=zmt[:, slot, :], in_=zmid[1 + ts:129 + ts, :]),
                inc=(f"ldm{slot}", 16), waits=[g_zmt[slot]])

            def w2mm(e, tt=tt, sub=sub):
                for j in range(22):
                    hj = min(128, HID - j * 128)
                    lh = htt[0:hj, j, sub * 128:(sub + 1) * 128]
                    e.matmul(psA[:, 0:512], lh, w2sb[0:hj, j, 0:512],
                             start=(j == 0), stop=False)
                    e.matmul(psA[:, 512:1024], lh, w2sb[0:hj, j, 512:1024],
                             start=(j == 0), stop=False)
                e.matmul(psA[:, 0:512], ones[0:1, 0:128], b2row[0:1, 0:512],
                         start=False, stop=True)
                return e.matmul(psA[:, 512:1024], ones[0:1, 0:128],
                                b2row[0:1, 512:1024], start=False, stop=True)
            vpe2 = b.run("tensor", w2mm, inc=("pe", 1),
                         waits=[vh if sub == 0 else None,
                                c_w2 if (tt == 0 and sub == 0) else None,
                                vzl, psg["psA"]])
            vpef3 = b.run("tensor", lambda e: e.matmul(
                psB[:, 0:8], ident[0:1, 0:128], ident[0:1, 0:8], start=True,
                stop=True), inc=("pe", 1))
            vz2 = b.run("vector", lambda e, slot=slot: e.tensor_add(
                z2t[:], psA[:], zmt[:, slot, :]), inc=("dve", 1),
                waits=[vpef3, vzl, st.get("g_z2t")])
            psg["psA"] = vz2
            g_zmt[slot] = vz2
            vzs = b.run("scalar", lambda e, ts=ts: e.dma_start(
                out=zout[1 + ts:129 + ts, :], in_=z2t[:]), inc=("stz2", 16),
                waits=[vz2])
            st["g_z2t"] = vzs
            if f"z{l + 1}" in debug:
                b.run("scalar", lambda e, ts=ts: e.dma_start(
                    out=dbg[f"z{l + 1}"][ts:ts + 128, :], in_=z2t[:]),
                    inc=("st", 16), waits=[vz2])
        g_htt = vpef3
    b.barrier(f"I7_{l}")
    b.free("zmt" + sfx, "z2t" + sfx, "htt" + sfx, "ytt" + sfx, "ggt" + sfx,
           "b1s" + sfx, "w1full" + sfx, "w2sb" + sfx, "b2row" + sfx)


def emit_final(b, st):
    H = st["H"]
    stt, trash, x1t, t1f = st["stt"], st["trash"], st["x1t"], st["t1f"]
    psA = st["psA"]
    z4t = b.sb("z4t", [128, 2, 1024], F32)
    g_z4 = [None, None]
    g_x1 = None
    for g in range(4 * NT5):
        ts = 128 * g
        slot = g % 2
        vz = b.run("sync", lambda e, ts=ts, slot=slot: e.dma_start(
            out=z4t[:, slot, :], in_=H["zs4"][1 + ts:129 + ts, :]),
            inc=(f"ldf{slot}", 16), waits=[g_z4[slot]])
        vx = b.run("sync", lambda e, ts=ts: e.dma_start(
            out=x1t[:], in_=H["x1s"][ts:ts + 128, :]), inc=("ldg", 16),
            waits=[g_x1])
        sl = stt[:, slot, :]
        w_rstd = ln_chain(b, st, z4t[:, slot, :], sl, trash,
                          waits=[vz], guard=g_z4[slot])
        va = b.run("scalar", lambda e, sl=sl, slot=slot: e.activation(
            t1f[:], z4t[:, slot, :], AF.Identity, bias=sl[:, 6:7],
            scale=sl[:, 5:6]), inc=("act", 1),
            waits=[w_rstd, st.get("g_t1f_f")])
        g_z4[slot] = va
        vo = b.run("vector", lambda e: e.tensor_add(x1t[:], x1t[:], t1f[:]),
                   inc=("dve", 1), waits=[va, vx])
        st["g_t1f_f"] = vo
        g_x1 = vo
        vs = b.run("sync", lambda e, ts=ts: e.dma_start(
            out=H["out"][ts:ts + 128, :], in_=x1t[:]), inc=("st", 16),
            waits=[vo])
        g_x1 = vs


# version 20
# speedup vs baseline: 1.1720x; 1.1720x over previous
import contextlib
import os
import numpy as np
import ml_dtypes
import concourse.bass as bass
import concourse.mybir as mybir
from concourse.bass_utils import run_bass_kernel_spmd

F32 = mybir.dt.float32
BF16 = mybir.dt.bfloat16
AF = mybir.ActivationFunctionType
AX = mybir.AxisListType
ALU = mybir.AluOpType

S = 4096
D = 1024
NT5 = S // 512
NEG = -1.0e9
HID = 2730
ENGS = ("tensor", "vector", "scalar", "gpsimd", "sync")


class B:

    def __init__(self, nc):
        self.nc = nc
        self.q = {e: [] for e in ENGS}
        self.stk = contextlib.ExitStack()
        self.sems = {}
        self.cnt = {}
        self.flush = None
        self.epoch = 0
        self.sem_epoch = {}

    def sem(self, name):
        if name not in self.sems:
            self.sems[name] = self.stk.enter_context(self.nc.semaphore(name))
            self.cnt[name] = 0
        return self.sems[name]

    def sb(self, name, shape, dtype):
        cm = self.nc.sbuf_tensor(name, shape, dtype)
        t = cm.__enter__()
        self.cms = getattr(self, "cms", {})
        self.seq = getattr(self, "seq", 0) + 1
        self.cms[name] = (self.seq, cm)
        return t

    def free(self, *names):
        for n in sorted(names, key=lambda n: -self.cms[n][0]):
            _, cm = self.cms.pop(n)
            cm.__exit__(None, None, None)

    def barrier(self, tag):
        vals = []
        fl = self.flush
        pre = {nm: self.cnt.get(nm, 0) for nm in ("pe", "dve", "act", "pool")}
        xwait = {"tensor": ("dve", pre["dve"]), "vector": ("act", pre["act"]),
                 "scalar": ("pool", pre["pool"]), "gpsimd": ("pe", pre["pe"])}
        for eng, nm in (("tensor", "pe"), ("vector", "dve"), ("scalar", "act"),
                        ("gpsimd", "pool")):
            v = self.bump(nm)
            sems = self.sems
            wn, wv = xwait[eng]
            self.sem(wn)

            def tk(e, nm=nm, eng=eng, wn=wn, wv=wv):
                if wv:
                    e.wait_ge(sems[wn], wv)
                if eng == "tensor" or fl is None:
                    e.nop().then_inc(sems[nm], 1)
                    if hasattr(e, "_n"):
                        e._n += 1
                elif eng == "scalar":
                    e.activation(fl[:, 2:4], fl[:, 2:4], AF.Copy)
                elif eng == "vector":
                    e.memset(fl[:, 0:2], 0.0)
                else:
                    e.memset(fl[:, 4:6], 0.0)
            self.q[eng].append(tk)
            vals.append((nm, v))
        dma_waits = [(nm, c) for nm, c in self.cnt.items()
                     if nm not in ("pe", "dve", "act", "pool", "bar")]
        bv = self.run("sync", lambda e: e.nop(), inc=("bar", 1),
                      waits=vals + dma_waits)
        for eng in ("tensor", "vector", "scalar", "gpsimd"):
            self.run(eng, lambda e: e.nop(), waits=[bv], nops=0)
        self.epoch += 1
        return bv

    def ps(self, name, shape):
        return self.stk.enter_context(self.nc.psum_tensor(name, shape, F32))

    def bump(self, name, by=1):
        self.sem(name)
        self.cnt[name] += by
        return self.cnt[name]

    ENGSEM = {"vector": "dve", "scalar": "act", "gpsimd": "pool"}

    def run(self, eng, f, inc=None, waits=(), nops=1):
        if (inc is not None and inc[1] >= 16
                and self.sem_epoch.get(inc[0], -1) < self.epoch
                and self.cnt.get(inc[0], 0) > 0):
            waits = list(waits) + [(inc[0], self.cnt[inc[0]])]
        if inc is not None and inc[1] >= 16:
            self.sem_epoch[inc[0]] = self.epoch
        flat = []
        for w in waits:
            if w is None:
                continue
            if isinstance(w, list):
                flat.extend(x for x in w if x is not None)
            else:
                flat.append(w)
        wl = tuple(w for w in flat if w[1])
        for nm_, v_ in wl:
            assert isinstance(nm_, str) and isinstance(v_, int), (
                f"bad wait {nm_!r} {v_!r}")
        esem = B.ENGSEM.get(eng)
        ret = None
        attach = None
        if esem is not None:
            n_data = nops if (inc is None or inc[0] == esem) else 0
            val = self.bump(esem, n_data) if n_data else self.cnt.get(esem, 0)
            if inc is not None:
                if inc[0] == esem:
                    ret = (esem, val)
                else:
                    attach = inc
                    ret = (inc[0], self.bump(*inc))
        else:
            if inc is not None:
                attach = inc
                ret = (inc[0], self.bump(*inc))
        for nm, _ in wl:
            self.sem(nm)
        sems = self.sems

        exp_after = self.cnt.get(esem, 0) if esem else None

        def thunk(e, f=f, attach=attach, wl=wl, exp=exp_after):
            for nm, v in wl:
                e.wait_ge(sems[nm], v)
            r = f(e)
            if attach is not None:
                r.then_inc(sems[attach[0]], attach[1])
            if exp is not None and hasattr(e, "_n") and e._n != exp:
                import traceback
                raise AssertionError(
                    f"nops drift: have {e._n} expected {exp} at thunk "
                    f"{getattr(f, '__code__', None) and f.__code__.co_filename}"
                    f":{f.__code__.co_firstlineno}")
        self.q[eng].append(thunk)
        return ret

    SELF_OPS = frozenset((
        "memset", "memzero", "tensor_copy", "tensor_tensor", "tensor_add",
        "tensor_sub", "tensor_mul", "tensor_max", "tensor_scalar",
        "tensor_single_scalar", "tensor_scalar_add", "tensor_scalar_sub",
        "tensor_scalar_mul", "tensor_scalar_max", "tensor_scalar_min",
        "tensor_relu", "tensor_reduce", "reduce_max", "reduce_sum",
        "scalar_tensor_tensor", "reciprocal", "select", "bn_stats",
        "bn_aggr", "pool", "pool_avg", "pool_max", "activation", "square",
        "sqrt", "mul", "copy", "add", "sign", "iota", "affine_select",
        "stream_shuffle"))

    class _SelfSync:
        def __init__(self, eng, sem):
            self._e = eng
            self._sem = sem
            self._n = 0

        def __getattr__(self, nm):
            real = getattr(self._e, nm)
            if nm in B.SELF_OPS:
                def call(*a, **k):
                    if self._n:
                        self._e.wait_ge(self._sem, self._n)
                    r = real(*a, **k)
                    r.then_inc(self._sem, 1)
                    self._n += 1
                    return r
                return call
            return real

    def finish(self, final_waits):
        for eng, nm, v in final_waits:
            self.sem(nm)
            sems = self.sems
            self.q[eng].append(lambda e, nm=nm, v=v: e.wait_ge(sems[nm], v))
        qq = self.q
        ss_dve = self.sem("dve")
        ss_act = self.sem("act")
        ss_pool = self.sem("pool")
        with self.nc.Block() as block:
            @block.tensor
            def _(t):
                for f in qq["tensor"]:
                    f(t)

            self._proxies = {}

            @block.vector
            def _(v):
                p = B._SelfSync(v, ss_dve)
                self._proxies["dve"] = p
                for f in qq["vector"]:
                    f(p)

            @block.scalar
            def _(s):
                p = B._SelfSync(s, ss_act)
                self._proxies["act"] = p
                for f in qq["scalar"]:
                    f(p)

            @block.gpsimd
            def _(g):
                p = B._SelfSync(g, ss_pool)
                self._proxies["pool"] = p
                for f in qq["gpsimd"]:
                    f(p)

            @block.sync
            def _(sy):
                for f in qq["sync"]:
                    f(sy)
        for nm, p in self._proxies.items():
            assert p._n == self.cnt.get(nm, 0), (
                f"self-sync drift on {nm}: emitted {p._n} ops, "
                f"build-time count {self.cnt.get(nm, 0)}")
        self.stk.close()


def make_consts():
    c = {}
    bf = ml_dtypes.bfloat16
    c["c_ident"] = np.eye(128, dtype=np.float32).astype(bf)
    c["c_ones"] = np.ones((1, 512), np.float32).astype(bf)
    i = np.arange(128)[:, None]
    j = np.arange(128)[None, :]
    c["c_maskA_T"] = np.where(j < i, NEG, 0.0).T.astype(np.float32).astype(bf)
    c["c_maskBo_T"] = np.where(j >= i, NEG, 0.0).T.astype(np.float32).astype(bf)
    c["c_maskBi_T"] = np.where(j > i, NEG, 0.0).T.astype(np.float32).astype(bf)
    t = np.arange(S)
    l = np.arange(64)
    lat = np.where(l[:, None] >= (t[None, :] // 64), NEG, 0.0).astype(np.float32)
    c["c_latmaskT"] = np.concatenate([lat, lat], 0).astype(bf)
    DH, W = 64, 512
    half = np.arange(0, DH, 2, dtype=np.float64)
    inv_freq = 1.0 / 10000.0 ** (half / DH)
    tp = np.arange(S, dtype=np.float64)
    fr = tp[:, None] * inv_freq[None, :]
    fr = np.concatenate([fr, fr], -1)
    base = (half + 0.4 * DH) / (1.4 * DH)
    power = (tp - S // 2) / (W // 2)
    xsc = base[None, :] ** power[:, None]
    xsc = np.concatenate([xsc, xsc], -1)
    sgn = np.where(np.arange(DH) < 32, -1.0, 1.0)[None, :]
    c["c_qA"] = (np.cos(fr) * xsc).astype(np.float32)
    c["c_qB"] = (sgn * np.sin(fr) * xsc).astype(np.float32)
    c["c_kA"] = (np.cos(fr) / xsc).astype(np.float32)
    c["c_kB"] = (sgn * np.sin(fr) / xsc).astype(np.float32)
    return c


WSHAPES = {
    "L0_attn_qkv_w": (D, 3 * D), "L0_attn_qkv_b": (3 * D,),
    "L0_attn_rq_w": (D, 256), "L0_attn_rq_b": (256,),
    "L0_attn_rkv_w": (D, 512), "L0_attn_rkv_b": (512,),
    "L0_attn_rout_w": (256, D), "L0_attn_rout_b": (D,),
    "L0_attn_out_w": (D, D), "L0_attn_out_b": (D,),
    "L0_attn_gl_w": (64, 1), "L0_attn_gr_w": (64, 1),
    "L0_ffn_wqkv": (2, D, 1536), "L0_ffn_q_scale": (2, 64),
    "L0_ffn_k_scale": (2, 64), "L0_ffn_wo": (2, 512, D),
    "L0_ffn_ln_g": (2, D), "L0_ffn_ln_b": (2, D),
    "L0_ffn_w1": (2, D, 5460), "L0_ffn_b1": (2, 5460),
    "L0_ffn_w2": (2, HID, D), "L0_ffn_b2": (2, D),
}
CONST_SHAPES = {
    "c_ident": (128, 128), "c_ones": (1, 512), "c_maskA_T": (128, 128),
    "c_maskBo_T": (128, 128), "c_maskBi_T": (128, 128),
    "c_latmaskT": (128, S), "c_qA": (S, 64), "c_qB": (S, 64),
    "c_kA": (S, 64), "c_kB": (S, 64),
}
CONST_DT = {k: BF16 for k in CONST_SHAPES}
for _n in ("c_qA", "c_qB", "c_kA", "c_kB"):
    CONST_DT[_n] = F32

DEBUG_SHAPES = {
    "ln1": ((S, D), BF16), "pooled": ((64, D), F32), "q": ((D, S), BF16),
    "k": ((D, S), BF16), "rq": ((256, S), BF16), "v": ((S, D), BF16),
    "localn": ((S, D), BF16), "latn": ((S, 256), BF16),
    "remote": ((S, D), BF16), "gate": ((S, 16), BF16),
    "mixed": ((S, D), BF16), "x1": ((S, D), F32), "z0": ((S, D), F32),
    "qk0": ((S, D), BF16), "qkr0": ((S, D), BF16), "vi0": ((S, 512), BF16),
    "o0": ((S, 512), BF16), "zwo0": ((S, D), F32), "y0": ((S, D), BF16),
    "hg0": ((128 * 22, S), BF16), "z1": ((S, D), F32),
    "yf1": ((1024, S), BF16),
    "qk1": ((S, D), BF16), "qkr1": ((S, D), BF16), "vi1": ((S, 512), BF16),
    "o1": ((S, 512), BF16), "zwo1": ((S, D), F32), "y1": ((S, D), BF16),
    "hg1": ((128 * 22, S), BF16), "z2": ((S, D), F32),
    "yf0": ((1024, S), BF16), "ys0": ((1024, S), BF16),
    "grep0": ((128, 1024), BF16), "brep0": ((128, 1024), BF16),
    "yfm0": ((128, 1024), BF16),
    "sc00": ((128, 1024), F32), "pr00": ((128, 512), BF16),
    "pT00": ((128, 512), BF16), "pv00": ((128, 1024), F32),
}


class ST(dict):
    pass


PHASE_A_FREE = [
    "wqk", "wv", "wrkv", "wrout", "wout", "bias_a", "bias_b",
    "glr", "grr", "rbrep", "obrep", "latmT", "maskBo", "xr", "ln1tm", "ln1fm",
    "pooledf", "pooledb", "qfm", "kfm", "rqfm", "vtm", "rkf", "rvf", "rvt",
    "probs", "probsT", "lprobs", "lprobsT", "nmx", "sexp", "RC", "RCL",
    "localn", "latn", "latfm", "remo", "lgl", "lgr", "gsig",
    "dtile", "mixfm",
]


def build(debug=(), phases=("A", "I", "F")):
    nc = bass.Bass("TRN2", target_bir_lowering=False)
    b = B(nc)
    H = {}
    H["L0_attn_x"] = nc.dram_tensor("L0_attn_x", [S, D], F32,
                                    kind="ExternalInput").ap()
    for n, sh in WSHAPES.items():
        H[n] = nc.dram_tensor(n, list(sh), F32, kind="ExternalInput").ap()
    for n, sh in CONST_SHAPES.items():
        H[n] = nc.dram_tensor(n, list(sh), CONST_DT[n], kind="ExternalInput").ap()
    H["out"] = nc.dram_tensor("out", [S, D], F32, kind="ExternalOutput").ap()
    H["x1s"] = nc.dram_tensor("x1s", [S, D], F32).ap()
    for zi in range(5):
        H[f"zs{zi}"] = nc.dram_tensor(f"zs{zi}", [S + 1, D], F32).ap()
    H["hs"] = nc.dram_tensor("hs", [22 * 128, S], BF16).ap()
    H["ysf"] = nc.dram_tensor("ysf", [8 * 128, S], BF16).ap()
    dbg = {}
    for name in debug:
        shp, dt = DEBUG_SHAPES[name]
        dbg[name] = nc.dram_tensor("dbg_" + name, list(shp), dt,
                                   kind="ExternalOutput").ap()
    st = ST()
    st["H"] = H
    st["dbg"] = dbg
    st["debug"] = set(debug)
    b.flush = b.sb("flushbuf", [128, 32], F32)
    b.run("gpsimd", lambda e: e.memset(b.flush[:], 0.0))
    emit_phase_A(b, st)
    b.barrier("A")
    b.free(*PHASE_A_FREE)
    if "I" in phases:
        for layer in range(2):
            emit_inner(b, st, layer)
    if "F" in phases:
        emit_final(b, st)
    b.finish([("sync", "st", b.cnt.get("st", 0))])
    return nc


def ln_chain(b, st, xap, sl, trash, W=1024, waits=(), guard=None):
    b.run("vector", lambda e: e.reduce_sum(sl[:, 0:1], xap, axis=AX.X,
                                           negate=True),
          waits=list(waits) + [guard])
    vd = b.run("vector", lambda e: e.tensor_scalar_mul(sl[:, 1:2], sl[:, 0:1],
                                                       1.0 / W), inc=("dve", 1))
    va = b.run("scalar", lambda e: e.activation(trash[:, 0:W], xap, AF.Square,
                                                bias=sl[:, 1:2], scale=1.0,
                                                accum_out=sl[:, 2:3]),
               inc=("act", 1), waits=[vd])
    vd2 = b.run("vector", lambda e: e.tensor_scalar(sl[:, 3:4], sl[:, 2:3],
                                                    1.0 / W, 1e-5, ALU.mult,
                                                    ALU.add),
                inc=("dve", 1), waits=[va])
    va2 = b.run("scalar", lambda e: e.activation(sl[:, 4:5], sl[:, 3:4], AF.Sqrt),
                inc=("act", 1), waits=[vd2])
    b.run("vector", lambda e: e.reciprocal(sl[:, 5:6], sl[:, 4:5]),
          waits=[va2])
    return b.run("vector", lambda e: e.tensor_mul(sl[:, 6:7], sl[:, 1:2],
                                                  sl[:, 5:6]), inc=("dve", 1))


def emit_phase_A(b, st):
    H = st["H"]
    dbg = st["dbg"]
    debug = st["debug"]

    ident = b.sb("ident", [128, 128], BF16)
    ones = b.sb("ones", [1, 512], BF16)
    maskA = b.sb("maskA", [128, 128], BF16)
    maskBi = b.sb("maskBi", [128, 128], BF16)
    stt = b.sb("stt", [128, 6, 8], F32)
    trash = b.sb("trash", [128, 1024], BF16)
    t1f = b.sb("t1f", [128, 1024], F32)
    t2f = b.sb("t2f", [128, 1024], F32)
    x1t = b.sb("x1t", [128, 1024], F32)
    maskBo = b.sb("maskBo", [128, 128], BF16)
    latmT = b.sb("latmT", [128, 2, 128], BF16)
    st["ident"], st["ones"], st["maskA"], st["maskBi"] = ident, ones, maskA, maskBi
    for sbuf, nm in ((ident, "c_ident"), (ones, "c_ones"), (maskA, "c_maskA_T"),
                     (maskBo, "c_maskBo_T"), (maskBi, "c_maskBi_T")):
        c_ld0 = b.run("sync", lambda e, s=sbuf, n=nm: e.dma_start(out=s[:],
                                                                  in_=H[n]),
                      inc=("cld", 16))
    st["c_consts"] = c_ld0

    wqk = b.sb("wqk", [128, 8, 2304], BF16)
    wv = b.sb("wv", [128, 8, 1024], BF16)
    wrkv = b.sb("wrkv", [128, 8, 512], BF16)
    wrout = b.sb("wrout", [128, 2, 1024], BF16)
    wout = b.sb("wout", [128, 8, 1024], BF16)
    bias_a = b.sb("bias_a", [1, 3328], BF16)
    bias_b = b.sb("bias_b", [1, 512], BF16)
    glr = b.sb("glr", [128, 1024], BF16)
    grr = b.sb("grr", [128, 1024], BF16)
    rbrep = b.sb("rbrep", [128, 1024], BF16)
    obrep = b.sb("obrep", [128, 1024], BF16)

    W_CUT = int(os.environ.get("W_CUT", "9"))
    qkvw = H["L0_attn_qkv_w"]
    for d in range(8 if W_CUT >= 2 else 0):
        b.run("gpsimd", lambda e, d=d: e.dma_start(
            out=wqk[:, d, 0:2048], in_=qkvw[d * 128:(d + 1) * 128, 0:2048]),
            inc=("wtA", 16))
        b.run("gpsimd", lambda e, d=d: e.dma_start(
            out=wqk[:, d, 2048:2304],
            in_=H["L0_attn_rq_w"][d * 128:(d + 1) * 128, :]), inc=("wtA", 16))
        b.run("gpsimd", lambda e, d=d: e.dma_start(
            out=wv[:, d, :], in_=qkvw[d * 128:(d + 1) * 128, 2048:3072]),
            inc=("wtA", 16))
        b.run("gpsimd", lambda e, d=d: e.dma_start(
            out=wrkv[:, d, :], in_=H["L0_attn_rkv_w"][d * 128:(d + 1) * 128, :]),
            inc=("wtA", 16))
        b.run("gpsimd", lambda e, d=d: e.dma_start(
            out=wout[:, d, :], in_=H["L0_attn_out_w"][d * 128:(d + 1) * 128, :]),
            inc=("wtA", 16))
    for c in range(2 if W_CUT >= 2 else 0):
        b.run("gpsimd", lambda e, c=c: e.dma_start(
            out=wrout[:, c, :], in_=H["L0_attn_rout_w"][c * 128:(c + 1) * 128, :]),
            inc=("wtA", 16))
    if W_CUT >= 3:
      b.run("gpsimd", lambda e: e.dma_start(out=bias_a[:, 0:3072],
                                            in_=H["L0_attn_qkv_b"][None, :]),
            inc=("wtA", 16))
      b.run("gpsimd", lambda e: e.dma_start(out=bias_a[:, 3072:3328],
                                            in_=H["L0_attn_rq_b"][None, :]),
            inc=("wtA", 16))
      b.run("gpsimd", lambda e: e.dma_start(out=bias_b[:, 0:512],
                                            in_=H["L0_attn_rkv_b"][None, :]),
            inc=("wtA", 16))

    if W_CUT < 4:
        c_wtA = ("wtA", b.cnt.get("wtA", 0))
        c_glr = None
        st["c_glr"] = None
    else:
      b.run("gpsimd", lambda e: e.dma_start(
          out=glr[:].rearrange("p (h j) -> p h j", j=64),
          in_=H["L0_attn_gl_w"].rearrange("a b -> (a b)")[None, None, :]
          .broadcast_to([128, 16, 64])), inc=("wtA", 16))
      b.run("gpsimd", lambda e: e.dma_start(
          out=grr[:].rearrange("p (h j) -> p h j", j=64),
          in_=H["L0_attn_gr_w"].rearrange("a b -> (a b)")[None, None, :]
          .broadcast_to([128, 16, 64])), inc=("wtA", 16))
      b.run("gpsimd", lambda e: e.dma_start(
          out=rbrep[:], in_=H["L0_attn_rout_b"][None, :]
          .broadcast_to([128, 1024])), inc=("wtA", 16))
      c_wtA = b.run("gpsimd", lambda e: e.dma_start(
          out=obrep[:], in_=H["L0_attn_out_b"][None, :]
          .broadcast_to([128, 1024])), inc=("wtA", 16))
      c_glr = c_wtA

      psA = b.ps("psA", [128, 1024])
      psB = b.ps("psB", [128, 1024])
      psPV = b.ps("psPV", [128, 1024])
      psL = b.ps("psL", [128, 1024])
      psLb = psL.bitcast(BF16)
      st["psA"], st["psB"], st["psPV"], st["psL"] = psA, psB, psPV, psL
      st["c_glr"] = c_glr

    xr = b.sb("xr", [128, 1, 1024], F32)
    ln1tm = b.sb("ln1tm", [128, 1, 1024], BF16)
    ln1fm = b.sb("ln1fm", [128, 2, 8, 512], BF16)
    pooledf = b.sb("pooledf", [128, 8, 64], F32)
    pooledb = b.sb("pooledb", [128, 8, 64], BF16)
    qfm = b.sb("qfm", [128, 2, 8, 512], BF16)
    kfm = b.sb("kfm", [128, 2, 8, 512], BF16)
    rqfm = b.sb("rqfm", [128, 2, 2, 512], BF16)
    vtm = b.sb("vtm", [128, 2, 4, 1024], BF16)
    rkf = b.sb("rkf", [128, 2, 64], BF16)
    rvf = b.sb("rvf", [128, 2, 64], BF16)
    rvt = b.sb("rvt", [128, 256], BF16)
    probs = b.sb("probs", [128, 2, 512], BF16)
    probsT = b.sb("probsT", [128, 2, 4, 128], BF16)
    lprobs = b.sb("lprobs", [128, 2, 128], BF16)
    lprobsT = b.sb("lprobsT", [128, 2, 128], BF16)
    nmx = b.sb("nmx", [128, 16], F32)
    sexp = b.sb("sexp", [128, 16], F32)
    RC = b.sb("RC", [128, 16], F32)
    RCL = b.sb("RCL", [128, 4], F32)
    localn = b.sb("localn", [128, 2, 1024], BF16)
    latn = b.sb("latn", [128, 2, 256], BF16)
    latfm = b.sb("latfm", [128, 2, 128], BF16)
    remo = b.sb("remo", [128, 1024], BF16)
    lgl = b.sb("lgl", [128, 16], F32)
    lgr = b.sb("lgr", [128, 16], F32)
    gsig = b.sb("gsig", [128, 16], BF16)
    dtile = b.sb("dtile", [128, 1024], BF16)
    mixfm = b.sb("mixfm", [128, 8, 128], BF16)

    st["stt"], st["trash"], st["x1t"], st["t1f"] = stt, trash, x1t, t1f
    A_CUT = int(os.environ.get("A_CUT", "10"))
    b.run("vector", lambda e: e.memset(probs[:], 0.0))
    b.run("vector", lambda e: e.memset(lprobs[:], 0.0))
    b.run("vector", lambda e: e.memset(rkf[:], 0.0))
    b.run("vector", lambda e: e.memset(rvf[:], 0.0))
    b.run("vector", lambda e: e.memset(pooledf[:], 0.0))
    vz = b.run("vector", lambda e: e.memset(t1f[:], 0.0), inc=("dve", 1))
    for zn in ("zs1", "zs3"):
        vzg = b.run("sync", lambda e, zn=zn: e.dma_start(out=H[zn][0:1, :],
                                                         in_=t1f[0:1, :]),
              inc=("stz0", 16), waits=[vz])
    st["g_t1f"] = vzg
    xin = H["L0_attn_x"]
    c_wt_full = b.cnt["wtA"]

    g_xr = [None, None]
    g_ln1tm = [None, None]
    ring_guard = [None, None]
    psg = {"psA": None, "psB": None, "psPV": None, "psL": None}
    g_probs = [None, None]
    g_probsT = [None, None]
    g_stA0 = [None, None]
    g_stPL = [None, None]
    g_stL2 = [None, None]
    qkv_evs = {}
    c_tr_sub = {}

    a1v_log = {}
    vpl_log = {}
    pool_log = {}

    def a0_block(i):
        r = i % 2
        for sub in range(4):
            g = 4 * i + sub
            slot = g % 2
            ts = 128 * g
            vld = b.run("sync", lambda e, ts=ts: e.dma_start(
                out=xr[:, 0, :], in_=xin[ts:ts + 128, :]),
                inc=("ldx0", 16), waits=[g_xr[0]])
            sl = stt[:, slot, :]
            w_rstd = ln_chain(b, st, xr[:, 0, :], sl, trash,
                              waits=[vld], guard=g_stA0[slot])
            va3 = b.run("scalar", lambda e, sl=sl: e.activation(
                ln1tm[:, 0, :], xr[:, 0, :], AF.Identity, bias=sl[:, 6:7],
                scale=sl[:, 5:6]), inc=("act", 1), waits=[w_rstd, g_ln1tm[0]])
            g_xr[0] = va3
            g_stA0[slot] = va3
            vtr = b.run("sync", lambda e, r=r, sub=sub: e.dma_start(
                out=ln1fm[:, r, :, sub * 128:(sub + 1) * 128],
                in_=ln1tm[:, 0, :], transpose=True), inc=(f"trl{r}", 16),
                waits=[va3, a1v_log.get(i - 2), vpl_log.get(i - 2)])
            g_ln1tm[0] = vtr
            c_tr_sub[g] = vtr
            if "ln1" in debug:
                vd_ln1 = b.run("scalar", lambda e, ts=ts: e.dma_start(
                    out=dbg["ln1"][ts:ts + 128, :], in_=ln1tm[:, 0, :]),
                    inc=("st", 16), waits=[vtr])
                g_ln1tm[0] = vd_ln1
        vpl = b.run("vector", lambda e, r=r, i=i: e.reduce_sum(
            pooledf[:, :, 8 * i:8 * (i + 1)].unsqueeze(3),
            ln1fm[:, r, :, :].rearrange("p c (a w) -> p c a w", w=64),
            axis=AX.X), inc=("dve", 1), waits=[c_tr_sub[4 * i + 3]])
        vplb = b.run("vector", lambda e, i=i: e.tensor_scalar_mul(
            pooledb[:, :, 8 * i:8 * (i + 1)], pooledf[:, :, 8 * i:8 * (i + 1)],
            1.0 / 64), inc=("dve", 1))
        vpl_log[i] = vpl
        pool_log[i] = vplb

    for i in range(NT5):
        if A_CUT < 2:
            break
        r = i % 2
        a0_block(i)
        vplb = pool_log[i]
        if A_CUT < 3:
            continue
        wt_w = ("wtA", c_wt_full) if i == 0 else None
        wt_c = st["c_consts"] if i == 0 else None
        ps_order = [psA[:, 0:512], psA[:, 512:1024], psB[:, 0:512],
                    psB[:, 512:1024]]
        names4 = ["psA", "psA", "psB", "psB"]
        slot_guard = [psg["psA"], psg["psA"], psg["psB"], psg["psB"]]
        evs = [None] * 18
        slot_guard_l = slot_guard

        def emit_a1_evac(fc, wait_v, pslot, dst, pap, scale):
            wv_ = [wait_v]
            if fc < 2:
                wv_.append(ring_guard[r])
            if fc % 2 == 0:
                if scale is None:
                    vev = b.run("vector", lambda e, dst=dst, pap=pap:
                                e.tensor_copy(dst, pap), inc=("dve", 1),
                                waits=wv_)
                else:
                    vev = b.run("vector", lambda e, dst=dst, pap=pap, s=scale:
                                e.tensor_scalar_mul(dst, pap, s),
                                inc=("dve", 1), waits=wv_)
            else:
                sc = 1.0 if scale is None else scale
                vev = b.run("scalar", lambda e, dst=dst, pap=pap, s=sc:
                            e.activation(dst, pap, AF.Copy, scale=s),
                            inc=("act", 1), waits=wv_)
            evs[fc] = vev
            slot_guard_l[pslot] = vev

        mm_log = []
        for fc in range(18):
            pslot = fc % 4
            pap = ps_order[pslot]
            fcol = fc * 128
            bcol = fcol if fc < 16 else 3072 + (fc - 16) * 128
            w = [wt_w, wt_c, c_tr_sub[4 * i + 3], slot_guard_l[pslot]]

            def mms(e, fcol=fcol, bcol=bcol, pap=pap, r=r):
                for d in range(8):
                    e.matmul(pap, wqk[:, d, fcol:fcol + 128], ln1fm[:, r, d, :],
                             start=(d == 0), stop=False)
                return e.matmul(pap, bias_a[0:1, bcol:bcol + 128], ones[0:1, :],
                                start=False, stop=True)
            vpe = b.run("tensor", mms, inc=("pe", 1), waits=w)
            if fc < 8:
                dst, scale = qfm[:, r, fc, :], 0.125
            elif fc < 16:
                dst, scale = kfm[:, r, fc - 8, :], None
            else:
                dst, scale = rqfm[:, r, fc - 16, :], 0.125
            mm_log.append((fc, vpe, pslot, dst, pap, scale))
            if fc >= 1:
                pf, pv, pp, pd, ppap, psc = mm_log[fc - 1]
                emit_a1_evac(pf, vpe, pp, pd, ppap, psc)
        psg["psA"] = [slot_guard[0], slot_guard[1]]
        psg["psB"] = [slot_guard[2], slot_guard[3]]
        qkv_evs[i] = evs
        v_log = []
        for sub in range(4):
            vps = psPV if sub % 2 == 0 else psL
            vkey = "psPV" if sub % 2 == 0 else "psL"
            w = [c_tr_sub[4 * i + sub], wt_w, psg[vkey]]
            if sub < 1:
                w.append(ring_guard[r])

            def vmm(e, r=r, sub=sub, vps=vps):
                for d in range(8):
                    e.matmul(vps[:, 0:512],
                             ln1fm[:, r, d, sub * 128:(sub + 1) * 128],
                             wv[:, d, 0:512], start=(d == 0), stop=False)
                    e.matmul(vps[:, 512:1024],
                             ln1fm[:, r, d, sub * 128:(sub + 1) * 128],
                             wv[:, d, 512:1024], start=(d == 0), stop=False)
                e.matmul(vps[:, 0:512], ones[0:1, 0:128], bias_a[0:1, 2048:2560],
                         start=False, stop=True)
                return e.matmul(vps[:, 512:1024], ones[0:1, 0:128],
                                bias_a[0:1, 2560:3072], start=False, stop=True)
            vpe = b.run("tensor", vmm, inc=("pe", 1), waits=w)
            if sub == 0:
                pf, pv, pp, pd, ppap, psc = mm_log[17]
                emit_a1_evac(pf, vpe, pp, pd, ppap, psc)
            else:
                pvpe, psub, pvps, pkey2 = v_log[-1]
                vev_ = b.run("vector", lambda e, r=r, psub=psub, pvps=pvps:
                             e.tensor_copy(vtm[:, r, psub, :], pvps[:]),
                             inc=("dve", 1), waits=[vpe])
                psg[pkey2] = vev_
            v_log.append((vpe, sub, vps, vkey))
        psg["psA"] = [slot_guard[0], slot_guard[1]]
        psg["psB"] = [slot_guard[2], slot_guard[3]]
        vfence0 = b.run("tensor", lambda e: e.matmul(
            psPV[:, 0:8], bias_a[0:1, 0:128], ones[0:1, 0:8], start=True,
            stop=True), inc=("pe", 1), waits=[psg["psPV"]])
        psg["psPV"] = vfence0
        pvpe, psub, pvps, pkey2 = v_log[-1]
        last_vev = b.run("vector", lambda e, r=r, psub=psub, pvps=pvps:
                         e.tensor_copy(vtm[:, r, psub, :], pvps[:]),
                         inc=("dve", 1), waits=[vfence0])
        psg[pkey2] = last_vev
        a1v_log[i] = v_log[-1][0]
        if "q" in debug:
            b.run("scalar", lambda e, r=r, i=i: e.dma_start(
                out=dbg["q"][:, i * 512:(i + 1) * 512]
                .rearrange("(c p) t -> p c t", p=128), in_=qfm[:, r, :, :]),
                inc=("st", 16), waits=evs[0:8])
        if "k" in debug:
            b.run("scalar", lambda e, r=r, i=i: e.dma_start(
                out=dbg["k"][:, i * 512:(i + 1) * 512]
                .rearrange("(c p) t -> p c t", p=128), in_=kfm[:, r, :, :]),
                inc=("st", 16), waits=evs[8:16])
        if "rq" in debug:
            b.run("scalar", lambda e, r=r, i=i: e.dma_start(
                out=dbg["rq"][:, i * 512:(i + 1) * 512]
                .rearrange("(c p) t -> p c t", p=128), in_=rqfm[:, r, :, :]),
                inc=("st", 16), waits=evs[16:18])
        if "v" in debug:
            b.run("scalar", lambda e, r=r, i=i: e.dma_start(
                out=dbg["v"][i * 512:(i + 1) * 512, :]
                .rearrange("(a p) d -> p a d", p=128), in_=vtm[:, r, :, :]),
                inc=("st", 16), waits=[last_vev])

        if A_CUT < 4:
            continue
        lat0 = 8 * i

        def rk_rv(e, lat0=lat0):
            last = None
            for c in range(2):
                pap = psL[:, c * 512:c * 512 + 8]
                for d in range(8):
                    e.matmul(pap, wrkv[:, d, c * 128:(c + 1) * 128],
                             pooledb[:, d, lat0:lat0 + 8], start=(d == 0),
                             stop=False)
                last = e.matmul(pap, bias_b[0:1, c * 128:(c + 1) * 128],
                                ones[0:1, 0:8], start=False, stop=True)
            for c in range(2):
                pap = psL[:, c * 512 + 256:c * 512 + 264]
                for d in range(8):
                    e.matmul(pap, wrkv[:, d, 256 + c * 128:256 + (c + 1) * 128],
                             pooledb[:, d, lat0:lat0 + 8], start=(d == 0),
                             stop=False)
                last = e.matmul(pap, bias_b[0:1, 256 + c * 128:256 + (c + 1) * 128],
                                ones[0:1, 0:8], start=False, stop=True)
            return last
        vpe = b.run("tensor", rk_rv, inc=("pe", 1),
                    waits=[vplb, wt_w, psg["psL"]])
        vfence = b.run("tensor", lambda e: e.matmul(
            psL[:, 8:16], bias_a[0:1, 0:128], ones[0:1, 0:8], start=True,
            stop=True), inc=("pe", 1))
        vev = b.run("vector", lambda e, i=i: (
            e.tensor_copy(rkf[:, 0, 8 * i:8 * i + 8], psL[:, 0:8]),
            e.tensor_copy(rkf[:, 1, 8 * i:8 * i + 8], psL[:, 512:520]),
            e.tensor_copy(rvf[:, 0, 8 * i:8 * i + 8], psL[:, 256:264]),
            e.tensor_copy(rvf[:, 1, 8 * i:8 * i + 8], psL[:, 768:776]))[-1],
            inc=("dve", 1), nops=4, waits=[vfence])

        def rvtr(e):
            e.transpose(psLb[0:64, 0:128], rvf[:, 0, :], ident[:])
            return e.transpose(psLb[0:64, 128:256], rvf[:, 1, :], ident[:])
        vpe2 = b.run("tensor", rvtr, inc=("pe", 1), waits=[vev])
        vrv = b.run("vector", lambda e: (
            e.tensor_copy(rvt[0:64, 0:128], psLb[0:64, 0:128]),
            e.tensor_copy(rvt[0:64, 128:256], psLb[0:64, 128:256]),
            e.tensor_copy(rvt[64:128, :], rvt[0:64, :]))[-1],
            inc=("dve", 1), nops=3, waits=[vpe2])
        psg["psL"] = vrv
        c_lat_ready = vrv

        if A_CUT < 5:
            continue
        last_pv = None
        def a2_headt(sub):
            g = 4 * i + sub
            ts = 128 * g
            lslot = g % 2
            pend_pv = None
            vex_last = None
            for hp in range(8):
                pspair = psA if hp % 2 == 0 else psB
                pkey = "psA" if hp % 2 == 0 else "psB"
                pslot = hp % 2
                kblocks = [g - 1, g] if g > 0 else [g]
                nb = len(kblocks)
                kaps = []
                for bk in kblocks:
                    ti = (bk // 4) % 2
                    kaps.append(kfm[:, ti, hp, (bk % 4) * 128:(bk % 4 + 1) * 128])
                w = [psg[pkey], evs[hp], evs[8 + hp]]
                if i > 0:
                    w.append(qkv_evs[i - 1][8 + hp])

                def smm(e, hp=hp, g=g, sub=sub, r=r, kaps=kaps, nb=nb,
                        pspair=pspair):
                    last = None
                    for he in range(2):
                        rows = slice(he * 64, (he + 1) * 64)
                        tp = (he * 64, 0)
                        base = he * 512
                        for jj, ka in enumerate(kaps):
                            e.matmul(pspair[:, base + jj * 128:base + (jj + 1) * 128],
                                     qfm[rows, r, hp, sub * 128:(sub + 1) * 128],
                                     ka[rows, :], start=(jj == 0), stop=False,
                                     tile_position=tp)
                        for jj in range(nb):
                            mk = maskBo if jj == nb - 1 else maskA
                            last = e.matmul(
                                pspair[:, base + jj * 128:base + (jj + 1) * 128],
                                mk[:], ident[:], start=False, stop=(jj == nb - 1))
                    return last
                vpe = b.run("tensor", smm, inc=("pe", 1), waits=w)
                if "sc00" in debug and g == 0 and hp == 0:
                    vdsc = b.run("vector", lambda e, pspair=pspair:
                                 e.tensor_copy(t1f[:], pspair[:]),
                                 inc=("dve", 1), waits=[vpe])
                    b.run("scalar", lambda e: e.dma_start(
                        out=dbg["sc00"], in_=t1f[:]), inc=("st", 16),
                        waits=[vdsc])
                vmx = b.run("vector", lambda e, hp=hp, pspair=pspair, nb=nb: (
                    e.reduce_max(nmx[:, 2 * hp:2 * hp + 1], pspair[:, 0:nb * 128],
                                 axis=AX.X, negate=True),
                    e.reduce_max(nmx[:, 2 * hp + 1:2 * hp + 2],
                                 pspair[:, 512:512 + nb * 128], axis=AX.X,
                                 negate=True))[-1], inc=("dve", 1),
                    nops=2, waits=[vpe, g_probs[pslot]])
                vex = b.run("scalar", lambda e, hp=hp, pslot=pslot, nb=nb,
                            pspair=pspair: (
                    e.activation(probs[:, pslot, 0:nb * 128],
                                 pspair[:, 0:nb * 128], AF.Exp,
                                 bias=nmx[:, 2 * hp:2 * hp + 1], scale=1.0,
                                 accum_out=sexp[:, 2 * hp:2 * hp + 1]),
                    e.activation(probs[:, pslot, 256:256 + nb * 128],
                                 pspair[:, 512:512 + nb * 128], AF.Exp,
                                 bias=nmx[:, 2 * hp + 1:2 * hp + 2], scale=1.0,
                                 accum_out=sexp[:, 2 * hp + 1:2 * hp + 2]))[-1],
                    inc=("act", 1), nops=2, waits=[vmx])
                psg[pkey] = vex
                vex_last = vex
                if "pr00" in debug and g == 0 and hp == 0:
                    b.run("scalar", lambda e, pslot=pslot: e.dma_start(
                        out=dbg["pr00"], in_=probs[:, pslot, :]),
                        inc=("st", 16), waits=[vex])
                vtp = b.run("sync", lambda e, pslot=pslot: e.dma_start(
                    out=probsT[:, pslot, :, :], in_=probs[:, pslot, :],
                    transpose=True), inc=(f"trp{pslot}", 16),
                    waits=[vex, g_probsT[pslot]])
                g_probs[pslot] = vtp

                def mk_pv(hp, pslot, vtp, kblocks=kblocks, nb=nb):
                    wpv = [vtp]
                    if hp == 0:
                        wpv += [psg["psPV"], st.get("g_pv2")]

                    def pvmm(e, hp=hp, pslot=pslot, kblocks=kblocks, nb=nb):
                        last = None
                        for he in range(2):
                            h = 2 * hp + he
                            for jj, bk in enumerate(kblocks):
                                ti = (bk // 4) % 2
                                vap = vtm[:, ti, bk % 4, 64 * h:64 * h + 64]
                                last = e.matmul(
                                    psPV[:, 64 * h:64 * h + 64],
                                    probsT[:, pslot, 2 * he + jj, :], vap,
                                    start=(h % 8 == 0 and jj == 0),
                                    stop=(h % 8 == 7 and jj == nb - 1))
                        return last
                    return pvmm, wpv
                if pend_pv is not None:
                    ppv, pw, ppslot = pend_pv
                    vpv = b.run("tensor", ppv, inc=("pe", 1), waits=pw)
                    g_probsT[ppslot] = vpv
                ppv, pw = mk_pv(hp, pslot, vtp)
                pend_pv = (ppv, pw, pslot)
            ppv, pw, ppslot = pend_pv
            vpv = b.run("tensor", ppv, inc=("pe", 1), waits=pw)
            g_probsT[ppslot] = vpv
            last_pv = vpv
            vrcA = b.run("vector", lambda e: e.reciprocal(RC[:, 0:16],
                                                          sexp[:, 0:16]),
                         inc=("dve", 1), waits=[vex_last])

            SKIP_LAT = os.environ.get("SKIP_LAT") == "1" or A_CUT < 6
            LAT_CUT = int(os.environ.get("LAT_CUT", "9"))
            Lg = 64
            mslot = g % 2
            if SKIP_LAT:
                vlm = None
            else:
                vlm = b.run("gpsimd", lambda e, g=g, mslot=mslot: e.dma_start(
                    out=latmT[:, mslot, :],
                    in_=H["c_latmaskT"][:, 128 * g:128 * (g + 1)]),
                    inc=(f"lmk{mslot}", 16), waits=[st.get(f"g_latm_{mslot}")])
            lat_ex = []
            for pg in range(2):
                if SKIP_LAT:
                    break
                w = [last_pv if pg == 0 else lat_ex[-1]]
                if pg == 0:
                    w += [c_lat_ready, psg["psL"]]

                def lsmm(e, pg=pg, g=g, sub=sub, r=r, Lg=Lg, mslot=mslot):
                    last = None
                    for he in range(2):
                        h = 2 * pg + he
                        rows = slice(he * 64, (he + 1) * 64)
                        tp = (he * 64, 0)
                        pap = psL[:, he * 512:he * 512 + Lg]
                        e.matmul(pap, rqfm[rows, r, pg, sub * 128:(sub + 1) * 128],
                                 rkf[rows, pg, 0:Lg], start=True, stop=False,
                                 tile_position=tp)
                        last = e.matmul(pap, latmT[rows, mslot, :],
                                        ident[rows, he * 64:he * 64 + Lg],
                                        start=False, stop=True, tile_position=tp)
                    return last
                vpe = b.run("tensor", lsmm, inc=("pe", 1),
                            waits=w + [vlm] if pg == 0 else w)
                if pg == 1:
                    st[f"g_latm_{mslot}"] = vpe
                vpef = b.run("tensor", lambda e: e.matmul(
                    psB[:, 8:16], bias_a[0:1, 0:128], ones[0:1, 0:8],
                    start=True, stop=True), inc=("pe", 1), waits=[psg["psB"]])
                psg["psB"] = vpef
                if LAT_CUT < 2:
                    lat_ex.append(vpef)
                    psg["psL"] = vpef
                    continue
                vmx = b.run("vector", lambda e, Lg=Lg: (
                    e.reduce_max(nmx[:, 0:1], psL[:, 0:Lg], axis=AX.X,
                                 negate=True),
                    e.reduce_max(nmx[:, 1:2], psL[:, 512:512 + Lg], axis=AX.X,
                                 negate=True))[-1], inc=("dve", 1),
                    nops=2, waits=[vpef])
                vex = b.run("scalar", lambda e, pg=pg, Lg=Lg: (
                    e.activation(lprobs[:, pg, 0:Lg], psL[:, 0:Lg], AF.Exp,
                                 bias=nmx[:, 0:1], scale=1.0,
                                 accum_out=sexp[:, 0:1]),
                    e.activation(lprobs[:, pg, 64:64 + Lg], psL[:, 512:512 + Lg],
                                 AF.Exp, bias=nmx[:, 1:2], scale=1.0,
                                 accum_out=sexp[:, 1:2]))[-1], inc=("act", 1),
                    nops=2, waits=[vmx, st.get("g_lprobs")])
                b.run("vector", lambda e, pg=pg: (
                    e.reciprocal(RCL[:, 2 * pg:2 * pg + 1], sexp[:, 0:1]),
                    e.reciprocal(RCL[:, 2 * pg + 1:2 * pg + 2],
                                 sexp[:, 1:2]))[-1], nops=2, waits=[vex])
                lat_ex.append(vex)
            if SKIP_LAT or LAT_CUT < 3:
                vtp = None
            elif True:
                vtp = b.run("sync", lambda e: e.dma_start(
                    out=lprobsT[:, :, :],
                    in_=lprobs[:, :, :].rearrange("p a c -> p (a c)"),
                    transpose=True), inc=("trlp", 16),
                    waits=[lat_ex[-1], st.get("g_lpT")])
                st["g_lprobs"] = vtp

            LPV_ON = (not SKIP_LAT) and LAT_CUT >= 4

            def lpv(e, Lg=Lg):
                last = None
                for h in range(4):
                    pg, he = h // 2, h % 2
                    rows = slice(he * 64, he * 64 + Lg)
                    col = 256 + 512 * (h % 2) + 64 * (h // 2)
                    last = e.matmul(psL[:, col:col + 64], lprobsT[rows, pg, :],
                                    rvt[rows, 64 * h:64 * h + 64],
                                    start=(h < 2), stop=(h >= 2))
                return last
            if SKIP_LAT or not LPV_ON:
                vpv = b.run("vector", lambda e: e.memset(latn[:], 0.0),
                            inc=("dve", 1), waits=[st.get("g_latn")])
            else:
                vpv = b.run("tensor", lpv, inc=("pe", 1),
                            waits=[vtp, lat_ex[-1]])
            st["g_lpT"] = vpv
            if SKIP_LAT or not LPV_ON or LAT_CUT == 4:
                vln = b.run("vector", lambda e: e.memset(latn[:], 0.0),
                            inc=("dve", 1),
                            waits=[st.get("g_latn"), vpv]) \
                    if LAT_CUT == 4 else vpv
            elif True:
                vpvf = b.run("tensor", lambda e: e.matmul(
                    psB[:, 8:16], bias_a[0:1, 0:128], ones[0:1, 0:8],
                    start=True, stop=True), inc=("pe", 1), waits=[psg["psB"]])
                psg["psB"] = vpvf
                latpv_view = psL[:].rearrange("p (x c) -> p x c", x=2)[:, :, 256:384] \
                    .rearrange("p x (h j) -> p x h j", j=64)
                vln = b.run("vector", lambda e, lv=latpv_view, lslot=lslot: e.tensor_tensor(
                    latn[:, lslot, :].rearrange("p (h a c) -> p a h c", a=2, c=64), lv,
                    RCL[:].rearrange("p (h a) -> p a h", a=2).unsqueeze(3)
                    .broadcast_to([128, 2, 2, 64]), ALU.mult),
                    inc=("dve", 1), waits=[vpvf, st.get(f"g_latn{lslot}")])
            if g == 0:
                vln = b.run("vector", lambda e, lslot=lslot: e.memset(
                    latn[0:64, lslot, :], 0.0), inc=("dve", 1))
            psg["psL"] = vln
            vloc = b.run("vector", lambda e, lslot=lslot: e.tensor_tensor(
                localn[:, lslot, :].rearrange("p (a c) -> p a c", c=64),
                psPV[:].rearrange("p (a c) -> p a c", c=64),
                RC[:].unsqueeze(2).broadcast_to([128, 16, 64]), ALU.mult),
                inc=("dve", 1), waits=[last_pv, st.get(f"g_localn{lslot}")])
            if g == 0:
                vloc = b.run("vector", lambda e, lslot=lslot: e.memset(
                    localn[0:1, lslot, :], 0.0), inc=("dve", 1))
            psg["psPV"] = vloc
            st["g_pv2"] = vloc
            if "localn" in debug:
                b.run("scalar", lambda e, ts=ts: e.dma_start(
                    out=dbg["localn"][ts:ts + 128, :], in_=localn[:, lslot, :]),
                    inc=("st", 16), waits=[vloc])
            if "latn" in debug:
                b.run("scalar", lambda e, ts=ts: e.dma_start(
                    out=dbg["latn"][ts:ts + 128, :], in_=latn[:, lslot, :]),
                    inc=("st", 16), waits=[vln])

            st["a_last_pv"] = last_pv
            return (sub, vln, vloc)

        def a2_tail2(sub, vln, vloc):
            g = 4 * i + sub
            ts = 128 * g
            lslot = g % 2
            if A_CUT < 7:
                return
            vtl = b.run("sync", lambda e, lslot=lslot: e.dma_start(
                out=latfm[:, :, :], in_=latn[:, lslot, :], transpose=True),
                inc=("trlf", 16), waits=[vln, st.get("g_latfm")])
            st[f"g_latn{lslot}"] = vtl

            def romm(e):
                last = None
                for c in range(2):
                    e.matmul(psA[:, 0:512], latfm[:, c, :], wrout[:, c, 0:512],
                             start=(c == 0), stop=(c == 1))
                    last = e.matmul(psA[:, 512:1024], latfm[:, c, :],
                                    wrout[:, c, 512:1024], start=(c == 0),
                                    stop=(c == 1))
                return last
            vro = b.run("tensor", romm, inc=("pe", 1),
                        waits=[vtl, psg["psA"]])
            st["g_latfm"] = vro
            vrof = b.run("tensor", lambda e: e.matmul(
                psB[:, 0:8], bias_a[0:1, 0:128], ones[0:1, 0:8], start=True,
                stop=True), inc=("pe", 1), waits=[psg["psB"]])
            psg["psB"] = vrof
            vrem = b.run("vector", lambda e: e.tensor_add(remo[:], psA[:],
                                                          rbrep[:]),
                         inc=("dve", 1), waits=[vrof, st.get("g_remo")])
            psg["psA"] = vrem

            if "remote" in debug:
                b.run("scalar", lambda e, ts=ts: e.dma_start(
                    out=dbg["remote"][ts:ts + 128, :], in_=remo[:]),
                    inc=("st", 16), waits=[vrem])

            if A_CUT < 8:
                return
            prod = dtile
            b.run("vector", lambda e, prod=prod, lslot=lslot: e.tensor_tensor(
                prod[:], localn[:, lslot, :], glr[:], ALU.mult),
                waits=([st["c_glr"]] if g == 0 and i == 0 else [])
                + [st.get("g_lgl"), st.get("g_x1t"), st.get("g_dtile")])
            b.run("vector", lambda e, prod=prod: e.reduce_sum(
                lgl[:].unsqueeze(2), prod[:].rearrange("p (a c) -> p a c", c=64),
                axis=AX.X))
            vp2 = b.run("vector", lambda e, prod=prod: e.tensor_tensor(
                prod[:], remo[:], grr[:], ALU.mult), waits=[vrem])
            b.run("vector", lambda e, prod=prod: e.reduce_sum(
                lgr[:].unsqueeze(2), prod[:].rearrange("p (a c) -> p a c", c=64),
                axis=AX.X))
            vga = b.run("vector", lambda e: e.tensor_add(lgl[:], lgl[:], lgr[:]),
                        inc=("dve", 1))
            vsg = b.run("scalar", lambda e: e.activation(gsig[:], lgl[:],
                                                         AF.Sigmoid),
                        inc=("act", 1), waits=[vga, st.get("g_gsig")])
            st["g_lgl"] = vsg
            if "gate" in debug:
                b.run("scalar", lambda e, ts=ts: e.dma_start(
                    out=dbg["gate"][ts:ts + 128, :], in_=gsig[:]),
                    inc=("st", 16), waits=[vsg])
            geng = "vector"
            gsem = "dve"
            vd_ = b.run(geng, lambda e, lslot=lslot: e.tensor_sub(
                dtile[:], localn[:, lslot, :], remo[:]),
                        inc=(gsem, 1), waits=[vsg])
            st[f"g_localn{lslot}"] = vd_
            b.run(geng, lambda e: e.tensor_tensor(
                dtile[:].rearrange("p (a c) -> p a c", c=64),
                dtile[:].rearrange("p (a c) -> p a c", c=64),
                gsig[:].unsqueeze(2).broadcast_to([128, 16, 64]), ALU.mult))
            vmix = b.run(geng, lambda e: e.tensor_add(dtile[:], dtile[:],
                                                      remo[:]),
                         inc=(gsem, 1))
            vmix = b.run(geng, lambda e: e.tensor_copy(
                b.flush[:, 24:32], dtile[:, 0:8]), inc=(gsem, 1))
            st["g_gsig"] = vmix
            if "mixed" in debug:
                b.run("scalar", lambda e, ts=ts: e.dma_start(
                    out=dbg["mixed"][ts:ts + 128, :], in_=dtile[:]),
                    inc=("st", 16), waits=[vmix])

            if A_CUT < 9:
                return
            vtm_ = b.run("sync", lambda e: e.dma_start(
                out=mixfm[:, :, :], in_=dtile[:], transpose=True),
                inc=("trmx", 16), waits=[vmix, st.get("g_mixfm")])
            st["g_dtile"] = vtm_
            st["g_remo"] = vmix
            vx0 = b.run("gpsimd", lambda e, ts=ts: e.dma_start(
                out=x1t[:], in_=xin[ts:ts + 128, :]), inc=("ldxr", 16),
                waits=[st.get("g_x1t")])

            def owmm(e):
                last = None
                for dch in range(8):
                    e.matmul(psB[:, 0:512], mixfm[:, dch, :], wout[:, dch, 0:512],
                             start=(dch == 0), stop=(dch == 7))
                    last = e.matmul(psB[:, 512:1024], mixfm[:, dch, :],
                                    wout[:, dch, 512:1024], start=(dch == 0),
                                    stop=(dch == 7))
                return last
            vow = b.run("tensor", owmm, inc=("pe", 1),
                        waits=[vtm_, psg["psB"]])
            st["g_mixfm"] = vow
            vowf = b.run("tensor", lambda e: e.matmul(
                psA[:, 0:8], bias_a[0:1, 0:128], ones[0:1, 0:8], start=True,
                stop=True), inc=("pe", 1), waits=[psg["psA"]])
            psg["psA"] = vowf
            vao = b.run("vector", lambda e: e.tensor_add(t2f[:], psB[:],
                                                         obrep[:]),
                        inc=("dve", 1), waits=[vowf, st.get("g_t2f")])
            psg["psB"] = vao
            slot = g % 2
            sl = stt[:, 2 + slot, :]
            w_rstd = ln_chain(b, st, t2f[:], sl, trash, waits=[vao],
                              guard=g_stPL[slot])
            vt1 = b.run("vector", lambda e, sl=sl: e.tensor_scalar(
                t1f[:], t2f[:], sl[:, 1:2], sl[:, 5:6], ALU.add, ALU.mult),
                inc=("dve", 1), waits=[w_rstd, vx0, st.get("g_t1f")])
            st["g_t2f"] = vt1
            g_stPL[slot] = vt1
            vx1 = b.run("vector", lambda e: e.tensor_add(x1t[:], x1t[:], t1f[:]),
                        inc=("dve", 1))
            vx1s = b.run("scalar", lambda e, ts=ts: e.dma_start(
                out=H["x1s"][ts:ts + 128, :], in_=x1t[:]), inc=("stx1", 16),
                waits=[vx1])
            if "x1" in debug:
                b.run("scalar", lambda e, ts=ts: e.dma_start(
                    out=dbg["x1"][ts:ts + 128, :], in_=x1t[:]), inc=("st", 16),
                    waits=[vx1])
            sl2 = stt[:, 4 + slot, :]
            w_rstd2 = ln_chain(b, st, x1t[:], sl2, trash, waits=[vx1],
                               guard=g_stL2[slot])
            va3 = b.run("scalar", lambda e, sl=sl2: e.activation(
                t1f[:], x1t[:], AF.Identity, bias=sl[:, 6:7], scale=sl[:, 5:6]),
                inc=("act", 1), waits=[w_rstd2])
            st["g_x1t"] = [va3, vx1s]
            g_stL2[slot] = va3
            vz0s = b.run("scalar", lambda e, ts=ts: e.dma_start(
                out=H["zs0"][1 + ts:129 + ts, :], in_=t1f[:]),
                inc=("stz0", 16), waits=[va3])
            st["g_t1f"] = vz0s
            if "z0" in debug:
                vz0d = b.run("scalar", lambda e, ts=ts: e.dma_start(
                    out=dbg["z0"][ts:ts + 128, :], in_=t1f[:]), inc=("st", 16),
                    waits=[va3])
                st["g_t1f"] = vz0d

        _a2p = []
        for sub in range(4):
            _h = a2_headt(sub)
            _a2p.append(_h)
            if len(_a2p) > 1:
                a2_tail2(*_a2p.pop(0))
        for _h in _a2p:
            a2_tail2(*_h)
        ring_guard[0] = st["a_last_pv"]
        ring_guard[1] = st["a_last_pv"]
        if "pooled" in debug and i == NT5 - 1:
            for cch in range(8):
                b.run("scalar", lambda e, cch=cch: e.dma_start(
                    out=dbg["pooled"].rearrange("l (c p) -> p c l", p=128)
                    [:, cch, :], in_=pooledf[:, cch, :],
                    allow_slow_non_contiguous=True),
                    inc=("st", 16), waits=[vplb])


def kernel(**inputs):
    consts = make_consts()
    nc = build()
    in_maps = []
    for c in range(8):
        m = {"L0_attn_x": np.ascontiguousarray(inputs["L0_attn_x"][c])}
        for n in WSHAPES:
            m[n] = np.ascontiguousarray(inputs[n], dtype=np.float32)
        m.update(consts)
        in_maps.append(m)
    res = run_bass_kernel_spmd(nc, in_maps, core_ids=list(range(8)))
    return np.stack([r["out"] for r in res.results]).astype(np.float32)


def emit_inner(b, st, l):
    H = st["H"]
    dbg = st["dbg"]
    debug = st["debug"]
    psA, psB, psPV, psL = st["psA"], st["psB"], st["psPV"], st["psL"]
    ident, ones = st["ident"], st["ones"]
    maskA, maskBi = st["maskA"], st["maskBi"]
    stt, trash, x1t, t1f = st["stt"], st["trash"], st["x1t"], st["t1f"]
    zin_name = f"zs{2 * l}"
    zmid_name = f"zs{2 * l + 1}"
    zout_name = f"zs{2 * l + 2}"
    L = l

    sfx = f"_{l}"
    b2row = b.sb("b2row" + sfx, [1, 1024], BF16)
    wqkv = b.sb("wqkv" + sfx, [128, 8, 1536], BF16)
    woi = b.sb("woi" + sfx, [128, 4, 1024], BF16)
    reps = b.sb("reps" + sfx, [128, 4, 64], BF16)
    grep = b.sb("grep" + sfx, [128, 1024], BF16)
    brep = b.sb("brep" + sfx, [128, 1024], BF16)
    for d in range(8):
        b.run("gpsimd", lambda e, d=d: e.dma_start(
            out=wqkv[:, d, :], in_=H["L0_ffn_wqkv"][L, d * 128:(d + 1) * 128, :]),
            inc=("wti", 16))
    for c in range(4):
        b.run("gpsimd", lambda e, c=c: e.dma_start(
            out=woi[:, c, :], in_=H["L0_ffn_wo"][L, c * 128:(c + 1) * 128, :]),
            inc=("wti", 16))
    c_wt0 = b.run("gpsimd", lambda e: e.dma_start(
        out=b2row[:], in_=H["L0_ffn_b2"][L][None, :]), inc=("wti", 16))

    qs = H["L0_ffn_q_scale"]
    ks = H["L0_ffn_k_scale"]
    b.run("gpsimd", lambda e: e.dma_start(
        out=reps[:, 0, :], in_=qs[L][None, :].broadcast_to([128, 64])),
        inc=("wti", 16))
    b.run("gpsimd", lambda e: e.dma_start(
        out=reps[:, 1, 0:32], in_=qs[L][None, 32:64].broadcast_to([128, 32])),
        inc=("wti", 16))
    b.run("gpsimd", lambda e: e.dma_start(
        out=reps[:, 1, 32:64], in_=qs[L][None, 0:32].broadcast_to([128, 32])),
        inc=("wti", 16))
    b.run("gpsimd", lambda e: e.dma_start(
        out=reps[:, 2, :], in_=ks[L][None, :].broadcast_to([128, 64])),
        inc=("wti", 16))
    b.run("gpsimd", lambda e: e.dma_start(
        out=reps[:, 3, 0:32], in_=ks[L][None, 32:64].broadcast_to([128, 32])),
        inc=("wti", 16))
    b.run("gpsimd", lambda e: e.dma_start(
        out=reps[:, 3, 32:64], in_=ks[L][None, 0:32].broadcast_to([128, 32])),
        inc=("wti", 16))
    b.run("gpsimd", lambda e: e.dma_start(
        out=grep[:], in_=H["L0_ffn_ln_g"][L][None, :].broadcast_to([128, 1024])),
        inc=("wti", 16))
    b.run("gpsimd", lambda e: e.dma_start(
        out=brep[:], in_=H["L0_ffn_ln_b"][L][None, :].broadcast_to([128, 1024])),
        inc=("wti", 16))
    c_grep = ("wti", b.cnt["wti"])
    c_reps = b.run("vector", lambda e: e.tensor_scalar_mul(
        reps[:, 0:2, :], reps[:, 0:2, :], 8.0), inc=("dve", 1),
        waits=[c_grep])

    zbf = b.sb("zbf" + sfx, [128, 4, 1024], BF16)
    zfm = b.sb("zfm" + sfx, [128, 2, 8, 528], BF16)
    qktm = b.sb("qktm" + sfx, [128, 2, 1024], BF16)
    tbl = b.sb("tbl" + sfx, [128, 2, 4, 64], BF16)
    ctb = b.sb("ctb" + sfx, [128, 2, 4, 64], BF16)
    tmp1 = b.sb("tmp1" + sfx, [128, 1024], BF16)
    tmp2 = b.sb("tmp2" + sfx, [128, 1024], BF16)
    qkrot = b.sb("qkrot" + sfx, [128, 2, 1024], BF16)
    qkfm = b.sb("qkfm" + sfx, [128, 2, 8, 512], BF16)
    vti = b.sb("vti" + sfx, [128, 2, 4, 512], BF16)
    iprobs = b.sb("iprobs" + sfx, [128, 2, 2, 640], BF16)
    iprobsT = b.sb("iprobsT" + sfx, [128, 2, 10, 128], BF16)
    inmx = b.sb("inmx" + sfx, [128, 8], F32)
    isexp = b.sb("isexp" + sfx, [128, 8], F32)
    RCi = b.sb("RCi" + sfx, [128, 8], F32)
    sqf = b.sb("sqf" + sfx, [128, 1024], F32)
    ssn = b.sb("ssn" + sfx, [128, 3, 16], F32)
    otm = b.sb("otm" + sfx, [128, 512], BF16)
    ofm = b.sb("ofm" + sfx, [128, 4, 128], BF16)
    zint = b.sb("zint" + sfx, [128, 2, 1024], F32)
    z1t = b.sb("z1t" + sfx, [128, 1024], F32)
    h2t = b.sb("h2t" + sfx, [128, 2, 1024], F32)
    zlast = b.sb("zlast" + sfx, [1, 512], F32)
    yt0 = b.sb("yt0" + sfx, [128, 1024], BF16)
    yfmt = b.sb("yfmt" + sfx, [128, 2, 8, 512], BF16)

    b.run("vector", lambda e: e.memset(iprobs[:], 0.0))
    g_zbf = [None, None, None, None]
    g_zfm = [None, None]
    ztr_log = {}
    g_qktm = [None, None]
    g_tbl = [None, None]
    g_qkrot = [None, None]
    ring_guard = [None, None]
    psg = {"psA": st.get("g_psA_inner"), "psB": None, "psPV": None, "psL":
           c_grep if False else None}
    g_iprobs = [None, None]
    g_iprobsT = [None, None]
    g_zint = [None, None]
    qk_evs = {}
    v_evs = {}
    g_stI = [None, None]

    zin = H[zin_name]
    zmid = H[zmid_name]
    zout = H[zout_name]
    ysf = H["ysf"]
    wt_all = ("wti", b.cnt["wti"])

    pv_log = {}
    smm_log = {}
    qkv_log = {}
    zgc_log = {}

    def sw_front(i, sub):
        r = i % 2
        rp = (i - 1) % 2
        g = 4 * i + sub
        ts = 128 * g
        slot = g % 2
        slot4 = g % 4
        vzb = b.run("gpsimd", lambda e, ts=ts, slot4=slot4: e.dma_start(
            out=zbf[:, slot4, :], in_=zin[1 + ts:129 + ts, :]),
            inc=(f"wtz{slot4}", 16), waits=[g_zbf[slot4]])
        vtr = b.run("sync", lambda e, r=r, sub=sub, slot4=slot4: e.dma_start(
            out=zfm[:, r, :, 16 + sub * 128:16 + (sub + 1) * 128],
            in_=zbf[:, slot4, :], transpose=True),
            inc=(f"trz{r}{sub}", 16),
            waits=[vzb, qkv_log.get(4 * (i - 2) + 3), zgc_log.get(i - 1)])
        g_zbf[slot4] = vtr
        ztr_log[(i, sub)] = vtr
        if sub == 0:
            if i == 0:
                vgc = b.run("vector", lambda e, r=r: e.memset(
                    zfm[:, r, :, 15:16], 0.0), inc=("dve", 1),
                    waits=[qkv_log.get(4 * (i - 2) + 3)])
            else:
                vgc = b.run("vector", lambda e, r=r, rp=rp: e.tensor_copy(
                    zfm[:, r, :, 15:16], zfm[:, rp, :, 527:528]),
                    inc=("dve", 1),
                    waits=[ztr_log[(i - 1, 3)],
                           qkv_log.get(4 * (i - 2) + 3)])
            g_zfm[r] = None
            st["_zgc"] = vgc
            zgc_log[i] = vgc
        w = [vtr, ztr_log.get((i, sub - 1)) if sub > 0 else None,
             st["_zgc"], wt_all if (i == 0 and sub == 0) else None,
             psg["psA"], psg["psB"]]

        def qkvmm(e, r=r, sub=sub):
            last = None
            for d in range(8):
                off = 16 + sub * 128 - (1 if d >= 4 else 0)
                lh = zfm[:, r, d, off:off + 128]
                e.matmul(psA[:, 0:512], lh, wqkv[:, d, 0:512],
                         start=(d == 0), stop=(d == 7))
                e.matmul(psA[:, 512:1024], lh, wqkv[:, d, 512:1024],
                         start=(d == 0), stop=(d == 7))
                last = e.matmul(psB[:, 0:512], lh, wqkv[:, d, 1024:1536],
                                start=(d == 0), stop=(d == 7))
            return last
        vpe = b.run("tensor", qkvmm, inc=("pe", 1), waits=w)
        qkv_log[g] = vpe
        vpef1 = b.run("tensor", lambda e: e.matmul(
            psPV[:, 0:8], ident[0:1, 0:128], ident[0:1, 0:8], start=True,
            stop=True), inc=("pe", 1), waits=[psg["psPV"]])
        psg["psPV"] = vpef1
        vqk = b.run("vector", lambda e, slot=slot: e.tensor_copy(
            qktm[:, slot, :], psA[:]), inc=("dve", 1),
            waits=[vpef1, g_qktm[slot]])
        psg["psA"] = vqk
        vvi = b.run("scalar", lambda e, r=r, sub=sub: e.activation(
            vti[:, r, sub, :], psB[:, 0:512], AF.Copy), inc=("act", 1),
            waits=[vpef1, pv_log.get(4 * i - 4 + sub)])
        psg["psB"] = vvi
        v_evs[(i, sub)] = vvi
        if f"qk{l}" in debug:
            b.run("scalar", lambda e, ts=ts, slot=slot: e.dma_start(
                out=dbg[f"qk{l}"][ts:ts + 128, :], in_=qktm[:, slot, :]),
                inc=("st", 16), waits=[vqk])
        if f"vi{l}" in debug:
            b.run("scalar", lambda e, ts=ts, r=r, sub=sub: e.dma_start(
                out=dbg[f"vi{l}"][ts:ts + 128, :], in_=vti[:, r, sub, :]),
                inc=("st", 16), waits=[vvi])

        for ci, cn in enumerate(("c_qA", "c_qB", "c_kA", "c_kB")):
            vtb = b.run("gpsimd", lambda e, ts=ts, slot=slot, ci=ci, cn=cn:
                        e.dma_start(out=tbl[:, slot, ci, :],
                                    in_=H[cn][ts:ts + 128, :]),
                        inc=(f"wtt{slot}", 16),
                        waits=[g_tbl[slot]] if ci == 0 else [])
        vct = b.run("vector", lambda e, slot=slot: e.tensor_tensor(
            ctb[:, slot, :, :], tbl[:, slot, :, :],
            reps[:, [0, 1, 2, 3], :] if False else reps[:, :, :], ALU.mult),
            inc=("dve", 1),
            waits=[vtb, c_reps if g == 0 else None])
        g_tbl[slot] = vct
        va = b.run("scalar", lambda e, slot=slot: e.activation(
            sqf[:], qktm[:, slot, :], AF.Square), inc=("act", 1),
            waits=[vqk, st.get("g_sqf")])
        b.run("vector", lambda e, slot=slot: e.tensor_tensor(
            tmp1[:, 0:512].rearrange("p (h c) -> p h c", c=64),
            qktm[:, slot, 0:512].rearrange("p (h c) -> p h c", c=64),
            ctb[:, slot, 0:1, :].broadcast_to([128, 8, 64]), ALU.mult),
            waits=[vct])
        b.run("vector", lambda e, slot=slot: e.tensor_tensor(
            tmp1[:, 512:1024].rearrange("p (h c) -> p h c", c=64),
            qktm[:, slot, 512:1024].rearrange("p (h c) -> p h c", c=64),
            ctb[:, slot, 2:3, :].broadcast_to([128, 8, 64]), ALU.mult))
        for base, tc in ((0, 1), (512, 3)):
            b.run("vector", lambda e, slot=slot, base=base, tc=tc: (
                e.tensor_tensor(
                    tmp2[:, base:base + 512]
                    .rearrange("p (h x) -> p h x", x=64)[:, :, 0:32],
                    qktm[:, slot, base + 32:base + 512 + 32]
                    .rearrange("p (h x) -> p h x", x=64)[:, :, 0:32]
                    if False else
                    qktm[:, slot, base:base + 512]
                    .rearrange("p (h x) -> p h x", x=64)[:, :, 32:64],
                    ctb[:, slot, tc:tc + 1, 0:32].broadcast_to([128, 8, 32]),
                    ALU.mult),
                e.tensor_tensor(
                    tmp2[:, base:base + 512]
                    .rearrange("p (h x) -> p h x", x=64)[:, :, 32:64],
                    qktm[:, slot, base:base + 512]
                    .rearrange("p (h x) -> p h x", x=64)[:, :, 0:32],
                    ctb[:, slot, tc:tc + 1, 32:64].broadcast_to([128, 8, 32]),
                    ALU.mult))[-1], nops=2)
        vt12 = b.run("vector", lambda e: e.tensor_add(tmp1[:], tmp1[:],
                                                      tmp2[:]),
                     inc=("dve", 1))
        vss = b.run("vector", lambda e, slot=slot: e.reduce_sum(
            ssn[:, 0, :].unsqueeze(2),
            sqf[:].rearrange("p (h c) -> p h c", c=64), axis=AX.X),
            inc=("dve", 1), waits=[va, g_stI[slot]])
        st["g_sqf"] = vss
        vsd = b.run("scalar", lambda e: e.activation(ssn[:, 1, :],
                                                     ssn[:, 0, :], AF.Sqrt),
                    inc=("act", 1), waits=[vss])
        b.run("vector", lambda e: e.tensor_scalar_max(ssn[:, 1, :],
                                                      ssn[:, 1, :], 1e-12),
              waits=[vsd])
        vrn = b.run("vector", lambda e: e.reciprocal(ssn[:, 2, :],
                                                     ssn[:, 1, :]),
                    inc=("dve", 1))
        vqr = b.run("vector", lambda e, slot=slot: e.tensor_tensor(
            qkrot[:, slot, :].rearrange("p (h c) -> p h c", c=64),
            tmp1[:].rearrange("p (h c) -> p h c", c=64),
            ssn[:, 2, :].unsqueeze(2).broadcast_to([128, 16, 64]), ALU.mult),
            inc=("dve", 1), waits=[vrn, g_qkrot[slot]])
        g_stI[slot] = vqr
        g_qktm[slot] = vqr
        if f"qkr{l}" in debug:
            b.run("scalar", lambda e, ts=ts, slot=slot: e.dma_start(
                out=dbg[f"qkr{l}"][ts:ts + 128, :], in_=qkrot[:, slot, :]),
                inc=("st", 16), waits=[vqr])
        vtr2 = b.run("sync", lambda e, r=r, sub=sub, slot=slot: e.dma_start(
            out=qkfm[:, r, :, sub * 128:(sub + 1) * 128],
            in_=qkrot[:, slot, :], transpose=True),
            inc=(f"trq{r}{sub}", 16),
            waits=[vqr, smm_log.get(4 * i - 4 + sub)])
        g_qkrot[slot] = vtr2
        qk_evs[(i, sub)] = vtr2


    def sw_back(i, sub):
        r = i % 2
        g = 4 * i + sub
        ts = 128 * g
        slot = g % 2

        kbl = [bk for bk in range(g - 4, g + 1) if bk >= 0]
        nb = len(kbl)
        pend_pv = None
        vex_last = None
        for q in range(4):
            pslot = q % 2
            w = [psg["psA"], psg["psB"], qk_evs[(i, sub)]]
            for bk in kbl:
                w.append(qk_evs[(bk // 4, bk % 4)])

            def smm(e, q=q, g=g, sub=sub, r=r, kbl=kbl, nb=nb):
                for jj, bk in enumerate(kbl):
                    sl_ = (bk // 4) % 2
                    for he, pspair in ((0, psA), (1, psB)):
                        rows = slice(he * 64, (he + 1) * 64)
                        ka = qkfm[rows, sl_, 4 + q,
                                  (bk % 4) * 128:(bk % 4 + 1) * 128]
                        e.matmul(pspair[:, jj * 128:(jj + 1) * 128],
                                 qkfm[rows, r, q, sub * 128:(sub + 1) * 128],
                                 ka, start=(jj == 0 or jj == 4), stop=False,
                                 tile_position=(he * 64, 0))
                for he, pspair in ((0, psA), (1, psB)):
                    if nb == 5:
                        e.matmul(pspair[:, 0:128], maskA[:], ident[:],
                                 start=False, stop=True)
                    e.matmul(pspair[:, (nb - 1) * 128:nb * 128],
                             maskBi[:], ident[:], start=False, stop=True)
                return e.matmul(psPV[:, 512:520], ident[0:1, 0:128],
                                ident[0:1, 0:8], start=True, stop=True)
            vpe = b.run("tensor", smm, inc=("pe", 1), waits=w)
            smm_log[g] = vpe
            vmx0 = b.run("vector", lambda e, q=q, nb=nb: e.reduce_max(
                inmx[:, 2 * q:2 * q + 1], psA[:, 0:nb * 128],
                axis=AX.X, negate=True),
                inc=("dve", 1), waits=[vpe, g_iprobs[pslot]])
            vmx1 = b.run("vector", lambda e, q=q, nb=nb: e.reduce_max(
                inmx[:, 2 * q + 1:2 * q + 2], psB[:, 0:nb * 128],
                axis=AX.X, negate=True), inc=("dve", 1))
            vex0 = b.run("scalar", lambda e, q=q, pslot=pslot, nb=nb:
                         e.activation(
                iprobs[:, pslot, 0, 0:nb * 128], psA[:, 0:nb * 128], AF.Exp,
                bias=inmx[:, 2 * q:2 * q + 1], scale=1.0,
                accum_out=isexp[:, 2 * q:2 * q + 1]), inc=("act", 1),
                waits=[vmx0])
            vex1 = b.run("scalar", lambda e, q=q, pslot=pslot, nb=nb:
                         e.activation(
                iprobs[:, pslot, 1, 0:nb * 128], psB[:, 0:nb * 128], AF.Exp,
                bias=inmx[:, 2 * q + 1:2 * q + 2], scale=1.0,
                accum_out=isexp[:, 2 * q + 1:2 * q + 2]), inc=("act", 1),
                waits=[vmx1])
            psg["psA"] = vex0
            psg["psB"] = vex1
            vex_last = vex1
            vtp = b.run("sync", lambda e, pslot=pslot: e.dma_start(
                out=iprobsT[:, pslot, :, :],
                in_=iprobs[:, pslot, :, :].rearrange("p a c -> p (a c)"),
                transpose=True), inc=(f"trip{pslot}", 16),
                waits=[vex1, g_iprobsT[pslot]])
            g_iprobs[pslot] = vtp

            def mk_pv(q, pslot, vtp, kbl=kbl, nb=nb):
                wpv = [vtp]
                if q == 0:
                    wpv += [psg["psPV"], v_evs[(i, sub)]]

                def pvmm(e, q=q, pslot=pslot, kbl=kbl, nb=nb):
                    last = None
                    for he in (0, 1):
                        h = 2 * q + he
                        for jj, bk in enumerate(kbl):
                            sl_ = (bk // 4) % 2
                            vap = vti[:, sl_, bk % 4, 64 * h:64 * h + 64]
                            last = e.matmul(
                                psPV[:, 64 * h:64 * h + 64],
                                iprobsT[:, pslot, he * 5 + jj, :], vap,
                                start=(h == 0 and jj == 0),
                                stop=(h == 7 and jj == nb - 1))
                    return last
                return pvmm, wpv
            if pend_pv is not None:
                ppv, pw, ppslot = pend_pv
                vpv = b.run("tensor", ppv, inc=("pe", 1), waits=pw)
                g_iprobsT[ppslot] = vpv
            ppv, pw = mk_pv(q, pslot, vtp)
            pend_pv = (ppv, pw, pslot)
        ppv, pw, ppslot = pend_pv
        vpv = b.run("tensor", ppv, inc=("pe", 1), waits=pw)
        g_iprobsT[ppslot] = vpv
        pv_log[g] = vpv
        last_pv = b.run("tensor", lambda e: e.matmul(
            psPV[:, 520:528], ident[0:1, 0:128], ident[0:1, 0:8],
            start=True, stop=True), inc=("pe", 1))
        vrc = b.run("vector", lambda e: e.reciprocal(RCi[:, 0:8],
                                                     isexp[:, 0:8]),
                    inc=("dve", 1), waits=[vex_last])
        voe = b.run("vector", lambda e: e.tensor_tensor(
            otm[:].rearrange("p (h c) -> p h c", c=64),
            psPV[:, 0:512].rearrange("p (h c) -> p h c", c=64),
            RCi[:].unsqueeze(2).broadcast_to([128, 8, 64]), ALU.mult),
            inc=("dve", 1), waits=[last_pv, st.get("g_otm")])
        psg["psPV"] = voe
        if f"o{l}" in debug:
            b.run("scalar", lambda e, ts=ts: e.dma_start(
                out=dbg[f"o{l}"][ts:ts + 128, :], in_=otm[:]),
                inc=("st", 16), waits=[voe])
        vto = b.run("sync", lambda e: e.dma_start(
            out=ofm[:, :, :], in_=otm[:], transpose=True),
            inc=("tro", 16), waits=[voe, st.get("g_ofm")])
        st["g_otm"] = vto

        vzl = b.run("sync", lambda e, ts=ts, slot=slot: e.dma_start(
            out=zint[:, slot, :], in_=zin[1 + ts:129 + ts, :]),
            inc=(f"ldz{slot}", 16), waits=[g_zint[slot]])

        def womm(e):
            last = None
            for c in range(4):
                e.matmul(psL[:, 0:512], ofm[:, c, :], woi[:, c, 0:512],
                         start=(c == 0), stop=(c == 3))
                last = e.matmul(psL[:, 512:1024], ofm[:, c, :],
                                woi[:, c, 512:1024],
                                start=(c == 0), stop=(c == 3))
            return last
        vwo = b.run("tensor", womm, inc=("pe", 1),
                    waits=[vto, psg["psL"]])
        st["g_ofm"] = vwo
        vwof = b.run("tensor", lambda e: e.matmul(
            psPV[:, 8:16], ident[0:1, 0:128], ident[0:1, 0:8], start=True,
            stop=True), inc=("pe", 1), waits=[psg["psPV"]])
        psg["psPV"] = vwof
        vz1 = b.run("vector", lambda e, slot=slot: e.tensor_add(
            z1t[:], psL[:], zint[:, slot, :]), inc=("dve", 1),
            waits=[vwof, vzl, st.get("g_z1t")])
        psg["psL"] = vz1
        g_zint[slot] = vz1
        vz1s = b.run("scalar", lambda e, ts=ts: e.dma_start(
            out=zmid[1 + ts:129 + ts, :], in_=z1t[:]),
            inc=(f"stm{slot}", 16), waits=[vz1])
        st["g_z1t"] = vz1s
        if f"zwo{l}" in debug:
            b.run("scalar", lambda e, ts=ts: e.dma_start(
                out=dbg[f"zwo{l}"][ts:ts + 128, :], in_=z1t[:]),
                inc=("st", 16), waits=[vz1])

        vh0 = b.run("vector", lambda e, slot=slot: e.tensor_copy(
            h2t[:, slot, 0:512], z1t[:, 0:512]), inc=("dve", 1),
            waits=[st.get(f"g_h2t{slot}")])
        vh1 = b.run("sync", lambda e, ts=ts, slot=slot: e.dma_start(
            out=h2t[:, slot, 512:1024], in_=zmid[ts:128 + ts, 512:1024]),
            inc=(f"ldh{slot}", 16),
            waits=[vz1s, st.get(f"g_stm{1 - slot}"), st.get(f"g_h2t{slot}")])
        st[f"g_stm{slot}"] = vz1s
        st["g_z1t"] = [st.get("g_z1t"), vh0]
        return (vh0, vh1)

    def sw_i5(i, sub, vh0, vh1):
        g = 4 * i + sub
        ts = 128 * g
        slot = g % 2
        sl = stt[:, slot, :]
        w_rstd = ln_chain(b, st, h2t[:, slot, :], sl, trash,
                          waits=[vh0, vh1],
                          guard=st.get(f"g_stI5_{slot}"))
        vya = b.run("scalar", lambda e, sl=sl, slot=slot: e.activation(
            yt0[:], h2t[:, slot, :], AF.Identity, bias=sl[:, 6:7],
            scale=sl[:, 5:6]),
            inc=("act", 1), waits=[w_rstd, st.get("g_yt0")])
        vya = b.run("scalar", lambda e, slot=slot: e.activation(
            b.flush[:, 16:24], h2t[:, slot, 0:8], AF.Copy), inc=("act", 1))
        st[f"g_stI5_{slot}"] = vya
        st[f"g_h2t{slot}"] = vya
        vyg = b.run("vector", lambda e: e.tensor_mul(yt0[:], yt0[:], grep[:]),
                    inc=("dve", 1),
                    waits=[vya,
                           c_grep if g == 0 else None])
        vyb = b.run("vector", lambda e: e.tensor_add(yt0[:], yt0[:], brep[:]),
                    inc=("dve", 1))
        vyb = b.run("vector", lambda e: e.tensor_copy(
            b.flush[:, 8:16], yt0[:, 0:8]), inc=("dve", 1))
        if f"y{l}" in debug:
            b.run("scalar", lambda e, ts=ts: e.dma_start(
                out=dbg[f"y{l}"][ts:ts + 128, :], in_=yt0[:]),
                inc=("st", 16), waits=[vyb])
        tslot = i % 2
        vty = b.run("scalar", lambda e, tslot=tslot, sub=sub: e.dma_start(
            out=yfmt[:, tslot, :, sub * 128:(sub + 1) * 128], in_=yt0[:],
            transpose=True), inc=(f"try{tslot}", 16),
            waits=[vyb, st.get(f"g_yfmt{tslot}") if sub == 0 else None])
        st["g_yt0"] = vty
        if sub == 3:
            vys = b.run("scalar", lambda e, i=i, tslot=tslot: e.dma_start(
                out=ysf.rearrange("(c p) t -> p c t", p=128)
                [:, :, i * 512:(i + 1) * 512],
                in_=yfmt[:, tslot, :, :]), inc=(f"sty{tslot}", 16),
                waits=[vty])
            st[f"g_yfmt{tslot}"] = vys
            st.setdefault(f"ydone{l}", {})[i] = [
                ("sty0", b.cnt.get("sty0", 0)),
                ("sty1", b.cnt.get("sty1", 0))]

    _pend = []
    _mids = []
    for i in range(NT5):
        for sub in range(4):
            sw_front(i, sub)
            _pend.append((i, sub))
            if len(_pend) > 1:
                mi, msub = _pend.pop(0)
                _mids.append((mi, msub) + sw_back(mi, msub))
            if len(_mids) > 1:
                sw_i5(*_mids.pop(0))
    mi, msub = _pend.pop(0)
    _mids.append((mi, msub) + sw_back(mi, msub))
    while _mids:
        sw_i5(*_mids.pop(0))

    c_y_done = [("sty0", b.cnt.get("sty0", 0)),
                ("sty1", b.cnt.get("sty1", 0))]
    if l == 0 and "grep0" in debug:
        b.run("scalar", lambda e: e.dma_start(
            out=dbg["grep0"], in_=grep[:]), inc=("st", 16), waits=c_y_done)
        b.run("scalar", lambda e: e.dma_start(
            out=dbg["brep0"], in_=brep[:]), inc=("st", 16), waits=c_y_done)
    if l == 0 and "yfm0" in debug:
        b.run("scalar", lambda e: e.dma_start(
            out=dbg["yfm0"].rearrange("p (c t) -> p c t", t=128),
            in_=yfmt[:, 1, :, :]), inc=("st", 16), waits=c_y_done)
    st["g_psA_inner"] = psg["psA"]
    b.barrier(f"sweep1_{l}")
    b.free("zbf" + sfx, "zfm" + sfx, "qktm" + sfx, "tbl" + sfx, "ctb" + sfx,
           "tmp1" + sfx, "tmp2" + sfx, "qkrot" + sfx, "qkfm" + sfx,
           "vti" + sfx, "iprobs" + sfx, "iprobsT" + sfx, "inmx" + sfx,
           "isexp" + sfx, "RCi" + sfx, "sqf" + sfx, "ssn" + sfx, "otm" + sfx,
           "ofm" + sfx, "zint" + sfx, "z1t" + sfx,
           "h2t" + sfx, "zlast" + sfx, "yt0" + sfx, "yfmt" + sfx,
           "reps" + sfx, "grep" + sfx, "brep" + sfx,
           "wqkv" + sfx, "woi" + sfx)

    w2sb = b.sb("w2sb" + sfx, [128, 22, 1024], BF16)
    w1full = b.sb("w1full" + sfx, [128, 8, 2 * HID], BF16)
    b1s = b.sb("b1s" + sfx, [1, 2, 2, 128], BF16)
    ggt = b.sb("ggt" + sfx, [128, 512], BF16)
    ytt = b.sb("ytt" + sfx, [128, 8, 512], BF16)
    htt = b.sb("htt" + sfx, [128, 22, 512], BF16)
    z2t = b.sb("z2t" + sfx, [128, 1024], F32)
    zmt = b.sb("zmt" + sfx, [128, 2, 1024], F32)
    for j in range(22):
        hj = min(128, HID - j * 128)
        b.run("gpsimd", lambda e, j=j, hj=hj: e.dma_start(
            out=w2sb[0:hj, j, :],
            in_=H["L0_ffn_w2"][L, j * 128:j * 128 + hj, :]), inc=("wtu", 16))
    c_w2 = ("wtu", b.cnt["wtu"])
    for d in range(8):
        b.run("gpsimd", lambda e, d=d: e.dma_start(
            out=w1full[:, d, :],
            in_=H["L0_ffn_w1"][L, d * 128:(d + 1) * 128, :]),
            inc=("wtw0", 16))
    c_w1 = ("wtw0", b.cnt["wtw0"])
    if l == 0 and "ys0" in debug:
        b.run("scalar", lambda e: e.dma_start(
            out=dbg["ys0"], in_=ysf[:, :]), inc=("st", 16), waits=c_y_done)
    g_ytt = None
    g_htt = None
    g_b1 = [None, None]
    g_zmt = [None, None]
    ydone = st[f"ydone{l}"]
    for tt in range(NT5):
        vyl = None
        for c in range(8):
            vyl = b.run("sync", lambda e, c=c, tt=tt: e.dma_start(
                out=ytt[:, c, :],
                in_=ysf[c * 128:(c + 1) * 128, tt * 512:(tt + 1) * 512]),
                inc=("ldy", 16),
                waits=(ydone[tt] + [g_ytt]) if c == 0 else [])
        vh = None
        for j in range(22):
            fw = min(128, HID - j * 128)
            slot = j % 2
            b.run("gpsimd", lambda e, j=j, fw=fw, slot=slot: e.dma_start(
                out=b1s[:, slot, 0, 0:fw],
                in_=H["L0_ffn_b1"][L][None, j * 128:j * 128 + fw]),
                inc=(f"wtb{slot}", 16), waits=[g_b1[slot]])
            vb1 = b.run("gpsimd", lambda e, j=j, fw=fw, slot=slot: e.dma_start(
                out=b1s[:, slot, 1, 0:fw],
                in_=H["L0_ffn_b1"][L][None, HID + j * 128:HID + j * 128 + fw]),
                inc=(f"wtb{slot}", 16))

            def w1mm(e, j=j, fw=fw, slot=slot, tt=tt):
                for d in range(8):
                    e.matmul(psA[0:fw, 0:512],
                             w1full[:, d, j * 128:j * 128 + fw],
                             ytt[:, d, :], start=(d == 0), stop=False)
                    e.matmul(psB[0:fw, 0:512],
                             w1full[:, d, HID + j * 128:HID + j * 128 + fw],
                             ytt[:, d, :], start=(d == 0), stop=False)
                e.matmul(psA[0:fw, 0:512], b1s[0:1, slot, 0, 0:fw],
                         ones[0:1, :], start=False, stop=True)
                return e.matmul(psB[0:fw, 0:512], b1s[0:1, slot, 1, 0:fw],
                                ones[0:1, :], start=False, stop=True)
            vpe = b.run("tensor", w1mm, inc=("pe", 1),
                        waits=[vb1, vyl if j == 0 else None,
                               c_w1 if (tt == 0 and j == 0) else None,
                               psg["psA"], psg["psB"]])
            g_b1[slot] = vpe
            vpef2 = b.run("tensor", lambda e: e.matmul(
                psPV[:, 0:8], ident[0:1, 0:128], ident[0:1, 0:8], start=True,
                stop=True), inc=("pe", 1))
            vgg = b.run("scalar", lambda e, fw=fw: e.activation(
                ggt[0:fw, :], psB[0:fw, 0:512], AF.Gelu), inc=("act", 1),
                waits=[vpef2])
            psg["psB"] = vgg
            vh = b.run("vector", lambda e, fw=fw, j=j: e.tensor_tensor(
                htt[0:fw, j, :], psA[0:fw, 0:512], ggt[0:fw, :], ALU.mult),
                inc=("dve", 1),
                waits=[vgg, g_htt if j == 0 else None])
            psg["psA"] = vh
        g_ytt = vpe
        for sub in range(4):
            g = 4 * tt + sub
            ts = 128 * g
            slot = g % 2
            vzl = b.run("sync", lambda e, ts=ts, slot=slot: e.dma_start(
                out=zmt[:, slot, :], in_=zmid[1 + ts:129 + ts, :]),
                inc=(f"ldm{slot}", 16), waits=[g_zmt[slot]])

            def w2mm(e, tt=tt, sub=sub):
                for j in range(22):
                    hj = min(128, HID - j * 128)
                    lh = htt[0:hj, j, sub * 128:(sub + 1) * 128]
                    e.matmul(psA[:, 0:512], lh, w2sb[0:hj, j, 0:512],
                             start=(j == 0), stop=False)
                    e.matmul(psA[:, 512:1024], lh, w2sb[0:hj, j, 512:1024],
                             start=(j == 0), stop=False)
                e.matmul(psA[:, 0:512], ones[0:1, 0:128], b2row[0:1, 0:512],
                         start=False, stop=True)
                return e.matmul(psA[:, 512:1024], ones[0:1, 0:128],
                                b2row[0:1, 512:1024], start=False, stop=True)
            vpe2 = b.run("tensor", w2mm, inc=("pe", 1),
                         waits=[vh if sub == 0 else None,
                                c_w2 if (tt == 0 and sub == 0) else None,
                                vzl, psg["psA"]])
            vpef3 = b.run("tensor", lambda e: e.matmul(
                psB[:, 0:8], ident[0:1, 0:128], ident[0:1, 0:8], start=True,
                stop=True), inc=("pe", 1))
            vz2 = b.run("vector", lambda e, slot=slot: e.tensor_add(
                z2t[:], psA[:], zmt[:, slot, :]), inc=("dve", 1),
                waits=[vpef3, vzl, st.get("g_z2t")])
            psg["psA"] = vz2
            g_zmt[slot] = vz2
            vzs = b.run("scalar", lambda e, ts=ts: e.dma_start(
                out=zout[1 + ts:129 + ts, :], in_=z2t[:]), inc=("stz2", 16),
                waits=[vz2])
            st["g_z2t"] = vzs
            if f"z{l + 1}" in debug:
                b.run("scalar", lambda e, ts=ts: e.dma_start(
                    out=dbg[f"z{l + 1}"][ts:ts + 128, :], in_=z2t[:]),
                    inc=("st", 16), waits=[vz2])
        g_htt = vpef3
    b.barrier(f"I7_{l}")
    b.free("zmt" + sfx, "z2t" + sfx, "htt" + sfx, "ytt" + sfx, "ggt" + sfx,
           "b1s" + sfx, "w1full" + sfx, "w2sb" + sfx, "b2row" + sfx)


def emit_final(b, st):
    H = st["H"]
    stt, trash, x1t, t1f = st["stt"], st["trash"], st["x1t"], st["t1f"]
    psA = st["psA"]
    z4t = b.sb("z4t", [128, 2, 1024], F32)
    g_z4 = [None, None]
    g_x1 = None
    for g in range(4 * NT5):
        ts = 128 * g
        slot = g % 2
        vz = b.run("sync", lambda e, ts=ts, slot=slot: e.dma_start(
            out=z4t[:, slot, :], in_=H["zs4"][1 + ts:129 + ts, :]),
            inc=(f"ldf{slot}", 16), waits=[g_z4[slot]])
        vx = b.run("sync", lambda e, ts=ts: e.dma_start(
            out=x1t[:], in_=H["x1s"][ts:ts + 128, :]), inc=("ldg", 16),
            waits=[g_x1])
        sl = stt[:, slot, :]
        w_rstd = ln_chain(b, st, z4t[:, slot, :], sl, trash,
                          waits=[vz], guard=g_z4[slot])
        va = b.run("scalar", lambda e, sl=sl, slot=slot: e.activation(
            t1f[:], z4t[:, slot, :], AF.Identity, bias=sl[:, 6:7],
            scale=sl[:, 5:6]), inc=("act", 1),
            waits=[w_rstd, st.get("g_t1f_f")])
        g_z4[slot] = va
        vo = b.run("vector", lambda e: e.tensor_add(x1t[:], x1t[:], t1f[:]),
                   inc=("dve", 1), waits=[va, vx])
        st["g_t1f_f"] = vo
        g_x1 = vo
        vs = b.run("sync", lambda e, ts=ts: e.dma_start(
            out=H["out"][ts:ts + 128, :], in_=x1t[:]), inc=("st", 16),
            waits=[vo])
        g_x1 = vs
